# version 03
# baseline (speedup 1.0000x reference)
import numpy as np
import ml_dtypes
import concourse.bass as bass
import concourse.mybir as mybir
from concourse.bass_utils import run_bass_kernel_spmd

F32 = mybir.dt.float32
F16 = mybir.dt.float16
BF16 = mybir.dt.bfloat16
AF = mybir.ActivationFunctionType
ALU = mybir.AluOpType
AX = mybir.AxisListType

NCORE = 8
TOK = 4096
NGRP = 2
GTOK = TOK // NGRP
NBLK = GTOK // 128
NTIL = GTOK // 512
D = 1024
DC = D // 128
HID = 2048
HC = HID // 128
NE = 8

XN_O = 0
WI_O = 16384
WO_O = 32768
HT_O = 49152
XNT_O = 57344
KSB_O = 59392
K_SB_O = KSB_O + 2080
V_SB_O = KSB_O + 3104
GFM_O = KSB_O + 2080
GTM_O = KSB_O + 4160
BROW_O = 65664
KVBF_O = 67712
BCB_O = 68752
ONE_O = 70800
IDT_O = 70928
ABF_COLS = 71056

XB_O = 0
LNS_O = 3072
LNS2_O = 4096
KVS_O = 5120
LNR_O = 6160
ONF_O = 7184
MV_O = 7312
RSD_O = 7344
RSD2_O = 7360
ST6_O = 7376
GATES_O = 7392
LOGI_O = 7520
M1_O = 7648
M2_O = 7664
MB_O = 7680
EQ_O = 7808
EX_O = 7936
SS_O = 8064
RH_O = 8080
DEN_O = 8096
RDN_O = 8112
BCL_O = 8128
AF32_COLS = 8160

AH_COLS = 16384


class Ctr:
    def __init__(self, h):
        self.h = h
        self.n = 0

    def inc(self, k=1):
        self.n += k
        return self.n

    def mark(self):
        return (self, self.n)


def _w(eng, mark):
    if mark is None:
        return
    if isinstance(mark, list):
        for m in mark:
            _w(eng, m)
        return
    ctr, val = mark
    assert val <= ctr.n, "wait on not-yet-emitted semaphore value"
    if val > 0:
        eng.wait_ge(ctr.h, val)


class Builder:
    pass


def build_program(phases=('A', 'B', 'C')):
    nc = bass.Bass("TRN2", target_bir_lowering=False, debug=False)

    dr = {}
    dr['xin'] = nc.dram_tensor("xcore", [TOK, D], F32, kind="ExternalInput")
    dr['xout'] = nc.dram_tensor("xout", [TOK, D], F32, kind="ExternalOutput")
    dr['w_in'] = nc.dram_tensor("w_in", [NE, D, HID], F32, kind="ExternalInput")
    dr['w_out'] = nc.dram_tensor("w_out", [NE, HID, D], F32, kind="ExternalInput")
    dr['wg1'] = nc.dram_tensor("wg1", [D, 256], F32, kind="ExternalInput")
    dr['wg2'] = nc.dram_tensor("wg2", [256, NE], F32, kind="ExternalInput")
    for w in ['wq', 'wk', 'wv', 'wh', 'wo_a']:
        dr[w] = nc.dram_tensor(w, [D, D], F32, kind="ExternalInput")
    dr['brows'] = nc.dram_tensor("brows", [128, 2048], F32, kind="ExternalInput")
    dr['lnrows'] = nc.dram_tensor("lnrows", [128, 1024], F32, kind="ExternalInput")
    dr['bcols'] = nc.dram_tensor("bcols", [128, 16], F32, kind="ExternalInput")
    dr['ident'] = nc.dram_tensor("ident", [128, 128], BF16, kind="ExternalInput")
    dr['x1'] = nc.dram_tensor("x1", [TOK, D], F32)
    dr['x2'] = nc.dram_tensor("x2", [TOK, D], F32)
    dr['xnscr'] = nc.dram_tensor("xnscr", [NGRP, 128, 16384], BF16)
    dr['cc_in'] = nc.dram_tensor("cc_in", [128, 1056], F32)
    dr['cc_out'] = nc.dram_tensor("cc_out", [128, 1056], F32)

    from contextlib import ExitStack
    with ExitStack() as st:
        B = Builder()
        B.nc = nc
        B.abf = st.enter_context(nc.sbuf_tensor("abf", [128, ABF_COLS], BF16))
        B.af = st.enter_context(nc.sbuf_tensor("af32", [128, AF32_COLS], F32))
        B.ah = st.enter_context(nc.sbuf_tensor("ah", [128, AH_COLS], F16))
        B.kvs = st.enter_context(nc.sbuf_tensor("kvs", [128, 1056], F32))
        B.pb = [st.enter_context(nc.psum_tensor(f"pb{i}", [128, 512], F32))
                for i in range(7)]
        B.pbt = st.enter_context(nc.psum_tensor("pbt", [128, 1024], BF16))
        B.pbx = B.pbt[:, :].bitcast(F32)
        B.cT = Ctr(st.enter_context(nc.semaphore("sT")))
        B.cV = Ctr(st.enter_context(nc.semaphore("sV")))
        B.cS = Ctr(st.enter_context(nc.semaphore("sS")))
        B.cG = Ctr(st.enter_context(nc.semaphore("sG")))
        B.cCC = Ctr(st.enter_context(nc.semaphore("sCC")))
        B.dC = Ctr(st.enter_context(nc.semaphore("dC")))
        B.dWI = Ctr(st.enter_context(nc.semaphore("dWI")))
        B.dWO = Ctr(st.enter_context(nc.semaphore("dWO")))
        B.dXB = [Ctr(st.enter_context(nc.semaphore(f"dXB{i}"))) for i in range(3)]
        B.dST = [Ctr(st.enter_context(nc.semaphore(f"dST{i}"))) for i in range(2)]
        B.dXN = Ctr(st.enter_context(nc.semaphore("dXN")))
        B.dCC = Ctr(st.enter_context(nc.semaphore("dCC")))
        B.block = st.enter_context(nc.Block())

        B.bank_free = {i: None for i in range(8)}
        B.xblk_free = [None, None, None]
        B.xnt_free = [None, None]
        B.lns_free = [None, None]
        B.gfm_free = [None, None]
        B.gtm_free = [None, None]
        B.wi_free = None
        B.wo_free = None
        B.wi_ready = None
        B.wo_ready = None
        B.xn_free = []
        B.aggr_prev = None
        B.src_barrier = []

        _setup(B, dr)
        if 'A' in phases:
            dst = dr['x1'] if ('B' in phases or 'C' in phases) else dr['xout']
            ffn_phase(B, dr, src=dr['xin'], dst=dst)
        if 'B' in phases or 'b' in phases:
            src = dr['x1'] if 'A' in phases else dr['xin']
            dst = dr['x2'] if 'C' in phases else dr['xout']
            attn_phase(B, dr, src=src, dst=dst, b1_only=('b' in phases))
        if 'C' in phases:
            src = dr['x2'] if 'B' in phases else (dr['x1'] if 'A' in phases else dr['xin'])
            ffn_phase(B, dr, src=src, dst=dr['xout'])

        @B.block.sync
        def _(sy):
            for c in [B.dC, B.dWI, B.dWO, B.dXN, B.dCC] + B.dXB + B.dST:
                if c.n:
                    sy.wait_ge(c.h, c.n)
            if B.cCC.n:
                sy.wait_ge(B.cCC.h, B.cCC.n)
    return nc


def _setup(B, dr):
    abf, af, pb = B.abf, B.af, B.pb
    block = B.block
    cT, cV, dC = B.cT, B.cV, B.dC

    @block.gpsimd
    def _(g):
        g.dma_start(out=abf[:, BROW_O:BROW_O + 2048], in_=dr['brows'].ap()).then_inc(dC.h, 16)
        g.dma_start(out=af[:, LNR_O:LNR_O + 1024], in_=dr['lnrows'].ap()).then_inc(dC.h, 16)
        g.dma_start(out=af[:, BCL_O:BCL_O + 16], in_=dr['bcols'].ap()).then_inc(dC.h, 16)
        g.dma_start(out=abf[:, IDT_O:IDT_O + 128], in_=dr['ident'].ap()).then_inc(dC.h, 16)
        g.dma_start(out=abf[:, KSB_O:KSB_O + 2048].rearrange("p (c g) -> p c g", c=DC),
                    in_=dr['wg1'].ap().rearrange("(c p) g -> p c g", p=128)).then_inc(dC.h, 16)
        g.dma_start(out=abf[:, KSB_O + 2048:KSB_O + 2064].rearrange("p (c g) -> p c g", c=2),
                    in_=dr['wg2'].ap().rearrange("(c p) g -> p c g", p=128)).then_inc(dC.h, 16)
    dC.inc(6 * 16)
    B.consts = dC.mark()

    @block.vector
    def _(v):
        v.memset(abf[:, ONE_O:ONE_O + 128], 1.0)
        v.memset(af[:, ONF_O:ONF_O + 128], 1.0).then_inc(cV.h, 1)
    ones_done = (cV, cV.inc())

    @block.tensor
    def _(t):
        _w(t, B.consts)
        _w(t, ones_done)
        for j, pbase in enumerate([0, 32]):
            for nt in range(2):
                t.matmul(pb[j * 2 + nt][:, 0:512],
                         af[pbase:pbase + 1, ONF_O:ONF_O + 128],
                         af[pbase:pbase + 1, LNR_O + nt * 512:LNR_O + (nt + 1) * 512],
                         start=True, stop=True).then_inc(cT.h, 1)
                cT.inc()
    bc_mm = cT.mark()

    @block.vector
    def _(v):
        _w(v, bc_mm)
        for j in range(2):
            for nt in range(2):
                ins = v.tensor_copy(abf[:, BCB_O + j * 1024 + nt * 512:BCB_O + j * 1024 + (nt + 1) * 512],
                                    pb[j * 2 + nt][:, 0:512])
        ins.then_inc(cV.h, 1)
    bc_done = (cV, cV.inc())
    for i in range(4):
        B.bank_free[i] = bc_done


def ln_pass(B, dr, src, g):
    abf, af, pbt = B.abf, B.af, B.pbt
    block = B.block
    cT, cV, cS = B.cT, B.cV, B.cS

    for blk in range(NBLK):
        slot = blk % 3
        dxb = B.dXB[slot]
        tok0 = g * GTOK + blk * 128

        @block.sync
        def _(sy, blk=blk, slot=slot, tok0=tok0):
            if blk == 0:
                _w(sy, B.src_barrier)
            _w(sy, B.xblk_free[slot])
            sy.dma_start(out=af[:, XB_O + slot * 1024:XB_O + (slot + 1) * 1024],
                         in_=src.ap()[tok0:tok0 + 128, :]).then_inc(dxb.h, 16)
        ld = (dxb, dxb.inc(16))

        @block.vector
        def _(v, blk=blk, slot=slot):
            _w(v, ld)
            if blk == 0:
                _w(v, B.consts)
            _w(v, B.aggr_prev)
            xb = af[:, XB_O + slot * 1024:XB_O + (slot + 1) * 1024]
            v.bn_stats(af[:, ST6_O:ST6_O + 6], xb[:, 0:512]).then_inc(cV.h, 1)
            v.wait_ge(cV.h, cV.inc())
            v.bn_stats(af[:, ST6_O + 6:ST6_O + 12], xb[:, 512:1024]).then_inc(cV.h, 1)
            v.wait_ge(cV.h, cV.inc())
            v.bn_aggr(af[:, MV_O + blk * 2:MV_O + blk * 2 + 2],
                      af[:, ST6_O:ST6_O + 12]).then_inc(cV.h, 1)
        aggr = (cV, cV.inc())
        B.aggr_prev = aggr
        B.xblk_free[slot] = aggr

    @block.scalar
    def _(s):
        _w(s, B.aggr_prev)
        s.activation(af[:, RSD_O:RSD_O + 16],
                     af[:, MV_O:MV_O + 32].rearrange("p (b two) -> p b two", two=2)[:, :, 1],
                     AF.Sqrt, bias=af[:, BCL_O + 10:BCL_O + 11], scale=1.0).then_inc(cS.h, 1)
    sq = (cS, cS.inc())

    @block.vector
    def _(v):
        _w(v, sq)
        v.reciprocal(af[:, RSD2_O:RSD2_O + 16], af[:, RSD_O:RSD_O + 16]).then_inc(cV.h, 1)
        v.wait_ge(cV.h, cV.inc())
    rstd_done = cV.mark()

    for blk in range(NBLK):
        slot = blk % 3
        dxb = B.dXB[slot]
        tok0 = g * GTOK + blk * 128

        @block.sync
        def _(sy, slot=slot, tok0=tok0):
            _w(sy, B.xblk_free[slot])
            sy.dma_start(out=af[:, XB_O + slot * 1024:XB_O + (slot + 1) * 1024],
                         in_=src.ap()[tok0:tok0 + 128, :]).then_inc(dxb.h, 16)
        ld = (dxb, dxb.inc(16))

        @block.vector
        def _(v, blk=blk, slot=slot):
            _w(v, ld)
            if blk == 0:
                _w(v, rstd_done)
            tslot = blk % 2
            _w(v, B.xnt_free[tslot])
            xb = af[:, XB_O + slot * 1024:XB_O + (slot + 1) * 1024]
            v.tensor_scalar(abf[:, XNT_O + tslot * 1024:XNT_O + (tslot + 1) * 1024],
                            xb, af[:, MV_O + blk * 2:MV_O + blk * 2 + 1],
                            af[:, RSD2_O + blk:RSD2_O + blk + 1],
                            op0=ALU.subtract, op1=ALU.mult).then_inc(cV.h, 1)
        norm = (cV, cV.inc())
        B.xblk_free[slot] = norm

        @block.tensor
        def _(t, blk=blk, g=g):
            tslot = blk % 2
            _w(t, norm)
            _w(t, B.bank_free[7])
            if blk == 0:
                _w(t, B.consts)
            for dc in range(DC):
                ins = t.transpose(pbt[:, dc * 128:(dc + 1) * 128],
                                  abf[:, XNT_O + tslot * 1024 + dc * 128:
                                      XNT_O + tslot * 1024 + (dc + 1) * 128],
                                  abf[:, IDT_O:IDT_O + 128])
            ins.then_inc(cT.h, 1)
        tp = (cT, cT.inc())

        @block.vector
        def _(v, blk=blk):
            _w(v, tp)
            if blk == 0:
                _w(v, B.xn_free)
            v.tensor_copy(abf[:, XN_O:XN_O + 16384]
                          .rearrange("p (c n) -> p c n", c=DC)[:, :, blk * 128:(blk + 1) * 128],
                          pbt[:, 0:1024].rearrange("p (c n) -> p c n", c=DC)).then_inc(cV.h, 1)
        ev = (cV, cV.inc())
        B.bank_free[7] = ev
        B.xnt_free[blk % 2] = ev
    B.xn_free = []
    return (cV, cV.n)


def ffn_phase(B, dr, src, dst):
    abf = B.abf
    block = B.block
    dWI, dWO = B.dWI, B.dWO
    B.src_barrier = [B.dST[0].mark(), B.dST[1].mark()]

    def dma_wi(e):
        @block.gpsimd
        def _(gp):
            _w(gp, B.wi_free)
            for half in range(2):
                gp.dma_start(
                    out=abf[:, WI_O + half * 8192:WI_O + (half + 1) * 8192]
                    .rearrange("p (c h) -> p c h", c=4),
                    in_=dr['w_in'].ap()[e].rearrange("(c p) h -> p c h", p=128)
                    [:, half * 4:(half + 1) * 4, :]).then_inc(dWI.h, 16)
        B.wi_ready = (dWI, dWI.inc(32))

    def dma_wo(e):
        @block.gpsimd
        def _(gp):
            _w(gp, B.wo_free)
            for half in range(2):
                gp.dma_start(
                    out=abf[:, WO_O + half * 8192:WO_O + (half + 1) * 8192]
                    .rearrange("p (c h) -> p c h", c=8),
                    in_=dr['w_out'].ap()[e].rearrange("(c p) h -> p c h", p=128)
                    [:, half * 8:(half + 1) * 8, :]).then_inc(dWO.h, 16)
        B.wo_ready = (dWO, dWO.inc(32))

    dma_wi(0)
    dma_wo(0)

    for g in range(NGRP):
        xn_ready = ln_pass(B, dr, src, g)
        gates_done = _emit_gating(B, g, xn_ready)
        for e in range(NE):
            _emit_expert(B, g, e, gates_done, xn_ready, dma_wi, dma_wo)
        _emit_residual(B, src, dst, g)


def _emit_gating(B, g, xn_ready):
    abf, af, pb = B.abf, B.af, B.pb
    block = B.block
    cT, cV, cS = B.cT, B.cV, B.cS

    gfm_marks = []
    for ti in range(NTIL):
        @block.tensor
        def _(t, ti=ti):
            if ti == 0:
                _w(t, xn_ready)
                _w(t, B.consts)
            _w(t, B.bank_free[2])
            _w(t, B.bank_free[3])
            for dc in range(DC):
                for mc in range(2):
                    ins = t.matmul(pb[2 + mc][:, 0:512],
                                   abf[:, KSB_O + dc * 256 + mc * 128:KSB_O + dc * 256 + (mc + 1) * 128],
                                   abf[:, XN_O + dc * 2048 + ti * 512:XN_O + dc * 2048 + (ti + 1) * 512],
                                   start=(dc == 0), stop=(dc == DC - 1))
            ins.then_inc(cT.h, 1)
        mm = (cT, cT.inc())

        @block.scalar
        def _(s, ti=ti):
            _w(s, mm)
            for mc in range(2):
                ins = s.activation(abf[:, HT_O + mc * 2048 + ti * 512:HT_O + mc * 2048 + (ti + 1) * 512],
                                   pb[2 + mc][:, 0:512], AF.Gelu,
                                   bias=af[:, BCL_O + 8 + mc:BCL_O + 9 + mc], scale=1.0)
            ins.then_inc(cS.h, 1)
        ev = (cS, cS.inc())
        B.bank_free[2] = ev
        B.bank_free[3] = ev
        gfm_marks.append(ev)

    @block.tensor
    def _(t):
        _w(t, gfm_marks)
        _w(t, B.bank_free[4])
        for blk in range(NBLK):
            for gc in range(2):
                ins = t.matmul(pb[4][:, blk * 8:(blk + 1) * 8],
                               abf[:, HT_O + gc * 2048 + blk * 128:HT_O + gc * 2048 + (blk + 1) * 128],
                               abf[:, KSB_O + 2048 + gc * 8:KSB_O + 2048 + (gc + 1) * 8],
                               start=(gc == 0), stop=(gc == 1))
        ins.then_inc(cT.h, 1)
    lg_mm = (cT, cT.inc())

    def vstep(v, ins):
        ins.then_inc(cV.h, 1)
        v.wait_ge(cV.h, cV.inc())

    @block.vector
    def _(v):
        _w(v, lg_mm)
        logi = af[:, LOGI_O:LOGI_O + 128]
        logi3 = logi.rearrange("p (b e) -> p b e", e=8)
        mb = af[:, MB_O:MB_O + 128]
        mb3 = mb.rearrange("p (b e) -> p b e", e=8)
        eq = af[:, EQ_O:EQ_O + 128]
        ex = af[:, EX_O:EX_O + 128]
        vstep(v, v.tensor_copy(logi, pb[4][:, 0:128]))
        vstep(v, v.tensor_reduce(af[:, M1_O:M1_O + 16], logi3, axis=AX.X, op=ALU.max))
        vstep(v, v.tensor_copy(mb3, af[:, M1_O:M1_O + 16].unsqueeze(2).broadcast_to([128, 16, 8])))
        vstep(v, v.tensor_tensor(eq, logi, mb, op=ALU.is_equal))
        vstep(v, v.scalar_tensor_tensor(ex, eq, -1e30, logi, op0=ALU.mult, op1=ALU.add))
        vstep(v, v.tensor_reduce(af[:, M2_O:M2_O + 16],
                                 ex.rearrange("p (b e) -> p b e", e=8), axis=AX.X, op=ALU.max))
        vstep(v, v.tensor_copy(mb3, af[:, M2_O:M2_O + 16].unsqueeze(2).broadcast_to([128, 16, 8])))
        vstep(v, v.tensor_tensor(eq, logi, mb, op=ALU.is_ge))
    pre_exp = cV.mark()

    @block.scalar
    def _(s):
        _w(s, pre_exp)
        s.activation(af[:, EX_O:EX_O + 128], af[:, LOGI_O:LOGI_O + 128], AF.Exp).then_inc(cS.h, 1)
    exp_done = (cS, cS.inc())

    @block.vector
    def _(v):
        _w(v, exp_done)
        ex = af[:, EX_O:EX_O + 128]
        mb = af[:, MB_O:MB_O + 128]
        vstep(v, v.tensor_reduce(af[:, SS_O:SS_O + 16],
                                 ex.rearrange("p (b e) -> p b e", e=8), axis=AX.X, op=ALU.add))
        vstep(v, v.tensor_scalar(af[:, SS_O:SS_O + 16], af[:, SS_O:SS_O + 16],
                                 2.0, None, op0=ALU.mult))
        vstep(v, v.reciprocal(af[:, RH_O:RH_O + 16], af[:, SS_O:SS_O + 16]))
        vstep(v, v.tensor_copy(mb.rearrange("p (b e) -> p b e", e=8),
                               af[:, RH_O:RH_O + 16].unsqueeze(2).broadcast_to([128, 16, 8])))
        vstep(v, v.tensor_tensor(af[:, LOGI_O:LOGI_O + 128], ex, mb, op=ALU.mult))
        v.tensor_tensor(af[:, GATES_O:GATES_O + 128], af[:, LOGI_O:LOGI_O + 128],
                        af[:, EQ_O:EQ_O + 128], op=ALU.mult).then_inc(cV.h, 1)
    gates_done = (cV, cV.inc())
    B.bank_free[4] = gates_done
    return gates_done


def _emit_expert(B, g, e, gates_done, xn_ready, dma_wi, dma_wo):
    abf, af, ah, pb = B.abf, B.af, B.ah, B.pb
    block = B.block
    cT, cV, cS = B.cT, B.cV, B.cS

    HP = [(0, 1), (2, 3)]
    OPAIR = [(4, 5), (6, 7)]
    wi_ready = B.wi_ready
    wo_ready = B.wo_ready
    last_h_mm = None
    last_o_mm = None

    for ti in range(NTIL):
        hevac = []
        for m in range(HC // 2):
            bkA, bkB = HP[m % 2]

            @block.tensor
            def _(t, ti=ti, m=m, bkA=bkA, bkB=bkB):
                if ti == 0 and m == 0:
                    _w(t, wi_ready)
                    _w(t, xn_ready)
                _w(t, B.bank_free[bkA])
                _w(t, B.bank_free[bkB])
                hc0, hc1 = 2 * m, 2 * m + 1
                rhs = abf[:, XN_O:XN_O + 16384]
                for dc in range(DC):
                    x_sl = abf[:, XN_O + dc * 2048 + ti * 512:XN_O + dc * 2048 + (ti + 1) * 512]
                    t.matmul(pb[bkA][:, 0:512],
                             abf[:, WI_O + dc * 2048 + hc0 * 128:WI_O + dc * 2048 + (hc0 + 1) * 128],
                             x_sl, start=(dc == 0), stop=(dc == DC - 1))
                    ins = t.matmul(pb[bkB][:, 0:512],
                                   abf[:, WI_O + dc * 2048 + hc1 * 128:WI_O + dc * 2048 + (hc1 + 1) * 128],
                                   x_sl, start=(dc == 0), stop=(dc == DC - 1))
                ins.then_inc(cT.h, 1)
            mm = (cT, cT.inc())
            if ti == NTIL - 1 and m == HC // 2 - 1:
                last_h_mm = mm

            @block.scalar
            def _(s, m=m, bkA=bkA, bkB=bkB):
                _w(s, mm)
                hc0, hc1 = 2 * m, 2 * m + 1
                s.activation(abf[:, HT_O + hc0 * 512:HT_O + (hc0 + 1) * 512],
                             pb[bkA][:, 0:512], AF.Gelu)
                s.activation(abf[:, HT_O + hc1 * 512:HT_O + (hc1 + 1) * 512],
                             pb[bkB][:, 0:512], AF.Gelu).then_inc(cS.h, 1)
            ev = (cS, cS.inc())
            B.bank_free[bkA] = ev
            B.bank_free[bkB] = ev
            hevac.append(ev)

        for sb in range(4):
            blkb = ti * 4 + sb
            pA, pB = OPAIR[sb % 2]

            @block.tensor
            def _(t, ti=ti, sb=sb, pA=pA, pB=pB):
                if sb == 0:
                    _w(t, hevac)
                    if ti == 0:
                        _w(t, wo_ready)
                        _w(t, gates_done)
                _w(t, B.bank_free[pA])
                _w(t, B.bank_free[pB])
                for kc in range(HC):
                    lhs = abf[:, HT_O + kc * 512 + sb * 128:HT_O + kc * 512 + (sb + 1) * 128]
                    t.matmul(_obank(B, pA), lhs,
                             abf[:, WO_O + kc * 1024:WO_O + kc * 1024 + 512],
                             start=(kc == 0), stop=(kc == HC - 1))
                    ins = t.matmul(_obank(B, pB), lhs,
                                   abf[:, WO_O + kc * 1024 + 512:WO_O + kc * 1024 + 1024],
                                   start=(kc == 0), stop=(kc == HC - 1))
                ins.then_inc(cT.h, 1)
            omm = (cT, cT.inc())
            if ti == NTIL - 1 and sb == 3:
                last_o_mm = omm

            @block.vector
            def _(v, blkb=blkb, pA=pA, pB=pB, e=e):
                _w(v, omm)
                gate = af[:, GATES_O + blkb * 8 + e:GATES_O + blkb * 8 + e + 1]
                for half, bk in ((0, pA), (1, pB)):
                    ya = ah[:, blkb * 1024 + half * 512:blkb * 1024 + (half + 1) * 512]
                    if e == 0:
                        ins = v.tensor_scalar(ya, _obank(B, bk), gate, None, op0=ALU.mult)
                    else:
                        ins = v.scalar_tensor_tensor(ya, _obank(B, bk), gate, ya,
                                                     op0=ALU.mult, op1=ALU.add)
                ins.then_inc(cV.h, 1)
            oev = (cV, cV.inc())
            B.bank_free[pA] = oev
            B.bank_free[pB] = oev

    B.wi_free = last_h_mm
    B.wo_free = last_o_mm
    if e == NE - 1:
        if g + 1 < NGRP:
            dma_wi(0)
            dma_wo(0)
    else:
        dma_wi(e + 1)
        dma_wo(e + 1)
    B.ya_done = cV.mark()
    B.xn_free = [last_h_mm]


def _obank(B, bk):
    return B.pbx if bk == 7 else B.pb[bk][:, 0:512]


def _emit_residual(B, src, dst, g):
    af, ah = B.af, B.ah
    block = B.block
    cV = B.cV
    ya_done = B.ya_done

    for blk in range(NBLK):
        slot = blk % 3
        dxb = B.dXB[slot]
        tok0 = g * GTOK + blk * 128

        @block.sync
        def _(sy, blk=blk, slot=slot, tok0=tok0):
            _w(sy, B.xblk_free[slot])
            sy.dma_start(out=af[:, XB_O + slot * 1024:XB_O + (slot + 1) * 1024],
                         in_=src.ap()[tok0:tok0 + 128, :]).then_inc(dxb.h, 16)
        ld = (dxb, dxb.inc(16))

        lslot = blk % 2
        lns = LNS_O if lslot == 0 else LNS2_O

        @block.vector
        def _(v, blk=blk, slot=slot, lns=lns, lslot=lslot):
            _w(v, ld)
            if blk == 0:
                _w(v, ya_done)
            _w(v, B.lns_free[lslot])
            v.tensor_tensor(af[:, lns:lns + 1024],
                            af[:, XB_O + slot * 1024:XB_O + (slot + 1) * 1024],
                            ah[:, blk * 1024:(blk + 1) * 1024], op=ALU.add).then_inc(cV.h, 1)
        add = (cV, cV.inc())
        B.xblk_free[slot] = add

        dst_sem = B.dST[lslot]

        @block.sync
        def _(sy, tok0=tok0, lns=lns):
            _w(sy, add)
            sy.dma_start(out=dst.ap()[tok0:tok0 + 128, :],
                         in_=af[:, lns:lns + 1024]).then_inc(dst_sem.h, 16)
        B.lns_free[lslot] = (dst_sem, dst_sem.inc(16))


def attn_phase(B, dr, src, dst, b1_only=False):
    B.src_barrier = [B.dST[0].mark(), B.dST[1].mark()]
    abf, af, pb, kvs = B.abf, B.af, B.pb, B.kvs
    block = B.block
    cT, cV, cS, cCC = B.cT, B.cV, B.cS, B.cCC
    dWI, dWO, dXN, dCC = B.dWI, B.dWO, B.dXN, B.dCC

    @block.gpsimd
    def _(gp):
        _w(gp, B.wo_free)
        for name, off in [('wk', WO_O), ('wv', WO_O + 8192)]:
            gp.dma_start(out=abf[:, off:off + 8192].rearrange("p (c h) -> p c h", c=DC),
                         in_=dr[name].ap().rearrange("(c p) h -> p c h", p=128)).then_inc(dWO.h, 16)
    wkv_ready = (dWO, dWO.inc(32))

    @block.gpsimd
    def _(gp):
        _w(gp, B.wi_free)
        for name, off in [('wq', WI_O), ('wh', WI_O + 8192)]:
            gp.dma_start(out=abf[:, off:off + 8192].rearrange("p (c h) -> p c h", c=DC),
                         in_=dr[name].ap().rearrange("(c p) h -> p c h", p=128)).then_inc(dWI.h, 16)
    wqh_ready = (dWI, dWI.inc(32))

    kv_marks = []
    kv_last_mm = None
    for g in range(NGRP):
        xn_ready = ln_pass(B, dr, src, g)

        @block.sync
        def _(sy, g=g):
            _w(sy, xn_ready)
            sy.dma_start(out=dr['xnscr'].ap()[g], in_=abf[:, XN_O:XN_O + 16384]).then_inc(dXN.h, 16)
        xn_saved = (dXN, dXN.inc(16))

        @block.vector
        def _(v, g=g):
            v.memset(kvs[:, g * 528:(g + 1) * 528], 0.0).then_inc(cV.h, 1)
        kv_acc = (cV, cV.inc())

        for blk in range(NBLK):
            @block.tensor
            def _(t, blk=blk):
                if blk == 0:
                    _w(t, wkv_ready)
                    _w(t, xn_ready)
                _w(t, B.bank_free[0])
                _w(t, B.bank_free[1])
                for dc in range(DC):
                    x_sl = abf[:, XN_O + dc * 2048 + blk * 128:XN_O + dc * 2048 + (blk + 1) * 128]
                    for half in range(2):
                        t.matmul(pb[half][:, 0:512], x_sl,
                                 abf[:, WO_O + dc * 1024 + half * 512:WO_O + dc * 1024 + (half + 1) * 512],
                                 start=(dc == 0), stop=False)
                for half in range(2):
                    ins = t.matmul(pb[half][:, 0:512],
                                   abf[0:1, ONE_O:ONE_O + 128],
                                   abf[0:1, BROW_O + half * 512:BROW_O + (half + 1) * 512],
                                   start=False, stop=True)
                ins.then_inc(cT.h, 1)
            kmm = (cT, cT.inc())

            @block.scalar
            def _(s):
                _w(s, kmm)
                for half in range(2):
                    ins = s.activation(abf[:, K_SB_O + half * 512:K_SB_O + (half + 1) * 512],
                                       pb[half][:, 0:512], AF.Sigmoid)
                ins.then_inc(cS.h, 1)
            kev = (cS, cS.inc())

            @block.tensor
            def _(t, blk=blk):
                _w(t, B.bank_free[2])
                _w(t, B.bank_free[3])
                for dc in range(DC):
                    x_sl = abf[:, XN_O + dc * 2048 + blk * 128:XN_O + dc * 2048 + (blk + 1) * 128]
                    for half in range(2):
                        t.matmul(pb[2 + half][:, 0:512], x_sl,
                                 abf[:, WO_O + 8192 + dc * 1024 + half * 512:
                                     WO_O + 8192 + dc * 1024 + (half + 1) * 512],
                                 start=(dc == 0), stop=False)
                for half in range(2):
                    ins = t.matmul(pb[2 + half][:, 0:512],
                                   abf[32:33, ONE_O:ONE_O + 128],
                                   abf[32:33, BROW_O + half * 512:BROW_O + (half + 1) * 512],
                                   start=False, stop=True)
                ins.then_inc(cT.h, 1)
            vmm = (cT, cT.inc())

            @block.scalar
            def _(s):
                _w(s, vmm)
                for half in range(2):
                    ins = s.copy(abf[:, V_SB_O + half * 512:V_SB_O + (half + 1) * 512],
                                 pb[2 + half][:, 0:512])
                ins.then_inc(cS.h, 1)
            vev = (cS, cS.inc())

            kvbank = 4 + (blk % 2)

            @block.tensor
            def _(t, blk=blk, kvbank=kvbank):
                _w(t, kev)
                _w(t, vev)
                _w(t, B.bank_free[kvbank])
                _w(t, B.bank_free[6])
                for h in range(16):
                    hp, par = h // 2, h % 2
                    lhs = abf[:, K_SB_O + h * 64:K_SB_O + (h + 1) * 64]
                    t.matmul(pb[kvbank][par * 64:par * 64 + 64, hp * 64:(hp + 1) * 64],
                             lhs, abf[:, V_SB_O + h * 64:V_SB_O + (h + 1) * 64],
                             start=True, stop=True, tile_position=(0, 64 * par))
                    ins = t.matmul(pb[6][par * 64:par * 64 + 64, (blk % 2) * 8 + hp:(blk % 2) * 8 + hp + 1],
                                   lhs, abf[:, ONE_O:ONE_O + 1],
                                   start=True, stop=True, tile_position=(0, 64 * par))
                ins.then_inc(cT.h, 1)
            kvmm = (cT, cT.inc())
            B.bank_free[0] = kvmm
            B.bank_free[1] = kvmm
            B.bank_free[2] = kvmm
            B.bank_free[3] = kvmm
            if g == NGRP - 1 and blk == NBLK - 1:
                kv_last_mm = kvmm

            @block.vector
            def _(v, blk=blk, g=g, kvbank=kvbank):
                _w(v, kvmm)
                v.tensor_tensor(kvs[:, g * 528:g * 528 + 512],
                                kvs[:, g * 528:g * 528 + 512],
                                pb[kvbank][:, 0:512], op=ALU.add)
                ins = v.tensor_tensor(kvs[:, g * 528 + 512:g * 528 + 520],
                                      kvs[:, g * 528 + 512:g * 528 + 520],
                                      pb[6][:, (blk % 2) * 8:(blk % 2) * 8 + 8], op=ALU.add)
                ins.then_inc(cV.h, 1)
            kv_acc = (cV, cV.inc())
            B.bank_free[kvbank] = kv_acc
            B.bank_free[6] = kv_acc

        kv_marks.append(kv_acc)
        B.xn_free = [kv_last_mm if kv_last_mm else kvmm, xn_saved]

    @block.sync
    def _(sy):
        _w(sy, kv_marks)
        sy.dma_start(out=dr['cc_in'].ap(), in_=kvs[:, 0:1056]).then_inc(dCC.h, 16)
    cc_st = (dCC, dCC.inc(16))

    @block.gpsimd
    def _(gp):
        _w(gp, cc_st)
        gp.collective_compute(
            "AllReduce", ALU.add,
            replica_groups=[list(range(NCORE))],
            ins=[dr['cc_in'].ap().opt()],
            outs=[dr['cc_out'].ap().opt()],
        ).then_inc(cCC.h, 1)
    cc_done = (cCC, cCC.inc())

    @block.sync
    def _(sy):
        _w(sy, cc_done)
        sy.dma_start(out=kvs[:, 0:1056], in_=dr['cc_out'].ap()).then_inc(dCC.h, 16)
    cc_ld = (dCC, dCC.inc(16))

    @block.vector
    def _(v):
        _w(v, cc_ld)
        for g in range(NGRP):
            v.tensor_copy(abf[:, KVBF_O + g * 520:KVBF_O + (g + 1) * 520]
                          .rearrange("p (hp c) -> p hp c", c=65)[:, :, 0:64],
                          kvs[:, g * 528:g * 528 + 512].rearrange("p (hp c) -> p hp c", c=64))
            ins = v.tensor_copy(abf[:, KVBF_O + g * 520:KVBF_O + (g + 1) * 520]
                                .rearrange("p (hp c) -> p hp c", c=65)[:, :, 64:65],
                                kvs[:, g * 528 + 512:g * 528 + 520].unsqueeze(2))
        ins.then_inc(cV.h, 1)
    kvbf = (cV, cV.inc())

    if b1_only:
        @block.sync
        def _(sy):
            _w(sy, kvbf)
            sy.dma_start(out=dst.ap()[0:128, :], in_=kvs[:, 0:1024]).then_inc(B.dST[0].h, 16)
            sy.dma_start(out=dst.ap()[128:256, 0:32], in_=kvs[:, 1024:1056]).then_inc(B.dST[0].h, 16)
        B.dST[0].inc(32)
        return

    @block.gpsimd
    def _(gp):
        _w(gp, [kv_last_mm])
        gp.dma_start(out=abf[:, WO_O:WO_O + 8192].rearrange("p (c h) -> p c h", c=DC),
                     in_=dr['wo_a'].ap().rearrange("(c p) h -> p c h", p=128)).then_inc(dWO.h, 16)
    woa_ready = (dWO, dWO.inc(16))

    for g in range(NGRP):
        _emit_b2(B, dr, src, dst, g, kvbf, wqh_ready, woa_ready)


def _emit_b2(B, dr, src, dst, g, kvbf, wqh_ready, woa_ready):
    abf, af, ah, pb, pbt = B.abf, B.af, B.ah, B.pb, B.pbt
    block = B.block
    cT, cV, cS = B.cT, B.cV, B.cS
    dXN = B.dXN

    def qfm(oc, c0, c1):
        if oc < 4:
            base = WO_O + 8192 + oc * 2048
        else:
            base = HT_O + (oc - 4) * 2048
        return abf[:, base + c0:base + c1]

    @block.sync
    def _(sy, g=g):
        _w(sy, B.xn_free)
        sy.dma_start(out=abf[:, XN_O:XN_O + 16384], in_=dr['xnscr'].ap()[g]).then_inc(dXN.h, 16)
    xn_ld = (dXN, dXN.inc(16))

    q_marks = []
    last_q_mm = None
    for oc in range(DC):
        for tp2 in range(NTIL // 2):
            tiA, tiB = 2 * tp2, 2 * tp2 + 1

            @block.tensor
            def _(t, oc=oc, tiA=tiA, tiB=tiB):
                if oc == 0 and tiA == 0:
                    _w(t, xn_ld)
                    _w(t, wqh_ready)
                _w(t, B.bank_free[5])
                _w(t, B.bank_free[6])
                for kc in range(DC):
                    w_sl = abf[:, WI_O + kc * 1024 + oc * 128:WI_O + kc * 1024 + (oc + 1) * 128]
                    t.matmul(pb[5][:, 0:512], w_sl,
                             abf[:, XN_O + kc * 2048 + tiA * 512:XN_O + kc * 2048 + (tiA + 1) * 512],
                             start=(kc == 0), stop=(kc == DC - 1))
                    ins = t.matmul(pb[6][:, 0:512], w_sl,
                                   abf[:, XN_O + kc * 2048 + tiB * 512:XN_O + kc * 2048 + (tiB + 1) * 512],
                                   start=(kc == 0), stop=(kc == DC - 1))
                ins.then_inc(cT.h, 1)
            mm = (cT, cT.inc())
            last_q_mm = mm

            @block.scalar
            def _(s, oc=oc, tiA=tiA, tiB=tiB):
                _w(s, mm)
                s.activation(qfm(oc, tiA * 512, (tiA + 1) * 512), pb[5][:, 0:512],
                             AF.Sigmoid, bias=af[:, BCL_O + oc:BCL_O + oc + 1], scale=1.0)
                s.activation(qfm(oc, tiB * 512, (tiB + 1) * 512), pb[6][:, 0:512],
                             AF.Sigmoid, bias=af[:, BCL_O + oc:BCL_O + oc + 1],
                             scale=1.0).then_inc(cS.h, 1)
            ev = (cS, cS.inc())
            B.bank_free[5] = ev
            B.bank_free[6] = ev
            q_marks.append(ev)

    for blk in range(NBLK):
        @block.tensor
        def _(t, blk=blk, g=g):
            if blk == 0:
                _w(t, kvbf)
                _w(t, q_marks)
            for j in range(4):
                _w(t, B.bank_free[j])
            ti, sb = blk // 4, blk % 4
            for hp in range(8):
                for par in range(2):
                    bank = par * 2 + hp // 4
                    s = hp % 4
                    lhs = qfm(hp, ti * 512 + sb * 128, ti * 512 + (sb + 1) * 128)
                    ins = t.matmul(
                        pb[bank][:, s * 65:s * 65 + 65],
                        lhs[par * 64:par * 64 + 64, :],
                        abf[par * 64:par * 64 + 64,
                            KVBF_O + g * 520 + hp * 65:KVBF_O + g * 520 + (hp + 1) * 65],
                        start=True, stop=True, tile_position=(64 * par, 0))
            ins.then_inc(cT.h, 1)
        nmm = (cT, cT.inc())

        @block.vector
        def _(v, blk=blk):
            _w(v, nmm)
            for j in range(4):
                ins = v.tensor_copy(af[:, DEN_O + j * 4:DEN_O + (j + 1) * 4],
                                    pb[j][:, 0:260].rearrange("p (s c) -> p s c", c=65)[:, :, 64])
            ins.then_inc(cV.h, 1)
            v.wait_ge(cV.h, cV.inc())
            v.reciprocal(af[:, RDN_O:RDN_O + 16], af[:, DEN_O:DEN_O + 16]).then_inc(cV.h, 1)
            v.wait_ge(cV.h, cV.inc())
            for h in range(8):
                par, hp = h % 2, h // 2
                bank = par * 2 + hp // 4
                s = hp % 4
                ins = v.tensor_scalar(ah[:, blk * 1024 + h * 64:blk * 1024 + (h + 1) * 64],
                                      pb[bank][:, s * 65:s * 65 + 64],
                                      af[:, RDN_O + bank * 4 + s:RDN_O + bank * 4 + s + 1],
                                      None, op0=ALU.mult)
            ins.then_inc(cV.h, 1)
        rdn_done = (cV, cV.inc())

        @block.scalar
        def _(s_, blk=blk):
            _w(s_, rdn_done)
            for h in range(8, 16):
                par, hp = h % 2, h // 2
                bank = par * 2 + hp // 4
                s = hp % 4
                ins = s_.activation(ah[:, blk * 1024 + h * 64:blk * 1024 + (h + 1) * 64],
                                    pb[bank][:, s * 65:s * 65 + 64], AF.Copy,
                                    scale=af[:, RDN_O + bank * 4 + s:RDN_O + bank * 4 + s + 1])
            ins.then_inc(cS.h, 1)
        attnev_s = (cS, cS.inc())

        @block.vector
        def _(v, blk=blk):
            _w(v, attnev_s)
            _w(v, B.aggr_prev)
            ab = ah[:, blk * 1024:(blk + 1) * 1024]
            v.bn_stats(af[:, ST6_O:ST6_O + 6], ab[:, 0:512]).then_inc(cV.h, 1)
            v.wait_ge(cV.h, cV.inc())
            v.bn_stats(af[:, ST6_O + 6:ST6_O + 12], ab[:, 512:1024]).then_inc(cV.h, 1)
            v.wait_ge(cV.h, cV.inc())
            v.bn_aggr(af[:, MV_O + blk * 2:MV_O + blk * 2 + 2],
                      af[:, ST6_O:ST6_O + 12]).then_inc(cV.h, 1)
        aggr = (cV, cV.inc())
        B.aggr_prev = aggr
        for j in range(4):
            B.bank_free[j] = aggr

    @block.scalar
    def _(s):
        _w(s, B.aggr_prev)
        s.activation(af[:, RSD_O:RSD_O + 16],
                     af[:, MV_O:MV_O + 32].rearrange("p (b two) -> p b two", two=2)[:, :, 1],
                     AF.Sqrt, bias=af[:, BCL_O + 10:BCL_O + 11], scale=1.0).then_inc(cS.h, 1)
    sq = (cS, cS.inc())

    @block.vector
    def _(v):
        _w(v, sq)
        v.reciprocal(af[:, RSD2_O:RSD2_O + 16], af[:, RSD_O:RSD_O + 16]).then_inc(cV.h, 1)
        v.wait_ge(cV.h, cV.inc())
    rstd_done = cV.mark()

    last_h_mm = None
    last_o_mm = None
    for blk in range(NBLK):
        tok0 = g * GTOK + blk * 128
        slot = blk % 3
        dxb = B.dXB[slot]

        @block.sync
        def _(sy, blk=blk, slot=slot, tok0=tok0):
            if blk == 0:
                _w(sy, B.src_barrier)
            _w(sy, B.xblk_free[slot])
            sy.dma_start(out=af[:, XB_O + slot * 1024:XB_O + (slot + 1) * 1024],
                         in_=src.ap()[tok0:tok0 + 128, :]).then_inc(dxb.h, 16)
        xld = (dxb, dxb.inc(16))

        lslot = blk % 2
        lns = LNS_O if lslot == 0 else LNS2_O
        lns2 = LNS2_O if lslot == 0 else LNS_O

        @block.vector
        def _(v, blk=blk, lns=lns, lslot=lslot):
            if blk == 0:
                _w(v, rstd_done)
            _w(v, B.lns_free[lslot])
            v.tensor_scalar(af[:, lns:lns + 1024], ah[:, blk * 1024:(blk + 1) * 1024],
                            af[:, MV_O + blk * 2:MV_O + blk * 2 + 1],
                            af[:, RSD2_O + blk:RSD2_O + blk + 1],
                            op0=ALU.subtract, op1=ALU.mult).then_inc(cV.h, 1)
            v.wait_ge(cV.h, cV.inc())
            _w(v, B.lns_free[1 - lslot])
            v.tensor_tensor(af[:, lns2:lns2 + 1024], af[:, lns:lns + 1024],
                            abf[:, BCB_O:BCB_O + 1024], op=ALU.mult).then_inc(cV.h, 1)
            v.wait_ge(cV.h, cV.inc())
            v.tensor_tensor(af[:, lns:lns + 1024], af[:, lns2:lns2 + 1024],
                            abf[:, BCB_O + 1024:BCB_O + 2048], op=ALU.add).then_inc(cV.h, 1)
        ln_done = (cV, cV.inc())

        @block.tensor
        def _(t, blk=blk):
            _w(t, B.bank_free[5])
            _w(t, B.bank_free[6])
            for half in range(2):
                t.matmul(pb[5 + half][:, 0:512],
                         abf[32:33, ONE_O:ONE_O + 128],
                         abf[32:33, BROW_O + 1024 + half * 512:BROW_O + 1024 + (half + 1) * 512],
                         start=True, stop=False)
            for dc in range(DC):
                x_sl = abf[:, XN_O + dc * 2048 + blk * 128:XN_O + dc * 2048 + (blk + 1) * 128]
                for half in range(2):
                    t.matmul(pb[5 + half][:, 0:512], x_sl,
                             abf[:, WI_O + 8192 + dc * 1024 + half * 512:
                                 WI_O + 8192 + dc * 1024 + (half + 1) * 512],
                             start=False, stop=False)
            for half in range(2):
                ins = t.matmul(pb[5 + half][:, 0:512],
                               abf[0:1, ONE_O:ONE_O + 128],
                               abf[0:1, BROW_O + 1024 + half * 512:BROW_O + 1024 + (half + 1) * 512],
                               start=False, stop=True)
            ins.then_inc(cT.h, 1)
        hmm = (cT, cT.inc())
        last_h_mm = hmm

        @block.scalar
        def _(s, blk=blk):
            _w(s, hmm)
            _w(s, B.xnt_free[blk % 2])
            for half in range(2):
                ins = s.copy(abf[:, XNT_O + (blk % 2) * 1024 + half * 512:
                                 XNT_O + (blk % 2) * 1024 + (half + 1) * 512],
                             pb[5 + half][:, 0:512])
            ins.then_inc(cS.h, 1)
        hev = (cS, cS.inc())
        B.bank_free[5] = hev
        B.bank_free[6] = hev

        @block.vector
        def _(v, blk=blk, lns=lns):
            _w(v, ln_done)
            _w(v, hev)
            _w(v, B.gtm_free[blk % 2])
            v.tensor_tensor(abf[:, GTM_O + (blk % 2) * 1024:GTM_O + (blk % 2 + 1) * 1024],
                            af[:, lns:lns + 1024],
                            abf[:, XNT_O + (blk % 2) * 1024:XNT_O + (blk % 2 + 1) * 1024],
                            op=ALU.mult).then_inc(cV.h, 1)
        gated = (cV, cV.inc())
        B.xnt_free[blk % 2] = gated
        lns_consumed = gated

        @block.tensor
        def _(t, blk=blk):
            _w(t, gated)
            _w(t, B.bank_free[7])
            for dc in range(DC):
                ins = t.transpose(pbt[:, dc * 128:(dc + 1) * 128],
                                  abf[:, GTM_O + (blk % 2) * 1024 + dc * 128:
                                      GTM_O + (blk % 2) * 1024 + (dc + 1) * 128],
                                  abf[:, IDT_O:IDT_O + 128])
            ins.then_inc(cT.h, 1)
        tp = (cT, cT.inc())
        B.gtm_free[blk % 2] = tp

        @block.vector
        def _(v, blk=blk):
            _w(v, tp)
            _w(v, B.gfm_free[blk % 2])
            v.tensor_copy(abf[:, GFM_O + (blk % 2) * 1024:GFM_O + (blk % 2 + 1) * 1024],
                          pbt[:, 0:1024]).then_inc(cV.h, 1)
        gfm = (cV, cV.inc())
        B.bank_free[7] = gfm

        @block.tensor
        def _(t, blk=blk):
            if blk == 0:
                _w(t, woa_ready)
            _w(t, gfm)
            _w(t, B.bank_free[5])
            _w(t, B.bank_free[6])
            for dc in range(DC):
                g_sl = abf[:, GFM_O + (blk % 2) * 1024 + dc * 128:
                           GFM_O + (blk % 2) * 1024 + (dc + 1) * 128]
                for half in range(2):
                    t.matmul(pb[5 + half][:, 0:512], g_sl,
                             abf[:, WO_O + dc * 1024 + half * 512:WO_O + dc * 1024 + (half + 1) * 512],
                             start=(dc == 0), stop=False)
            for half in range(2):
                ins = t.matmul(pb[5 + half][:, 0:512],
                               abf[64:65, ONE_O:ONE_O + 128],
                               abf[64:65, BROW_O + half * 512:BROW_O + (half + 1) * 512],
                               start=False, stop=True)
            ins.then_inc(cT.h, 1)
        omm = (cT, cT.inc())
        last_o_mm = omm
        B.gfm_free[blk % 2] = omm

        @block.scalar
        def _(s, blk=blk, lns2=lns2):
            _w(s, omm)
            _w(s, B.lns_free[1 - (blk % 2)])
            for half in range(2):
                ins = s.activation(af[:, lns2 + half * 512:lns2 + (half + 1) * 512],
                                   pb[5 + half][:, 0:512], AF.Relu)
            ins.then_inc(cS.h, 1)
        relu = (cS, cS.inc())
        B.bank_free[5] = relu
        B.bank_free[6] = relu

        @block.vector
        def _(v, blk=blk, slot=slot, lns=lns, lns2=lns2):
            _w(v, relu)
            _w(v, xld)
            _w(v, lns_consumed)
            v.tensor_tensor(af[:, lns:lns + 1024],
                            af[:, XB_O + slot * 1024:XB_O + (slot + 1) * 1024],
                            af[:, lns2:lns2 + 1024], op=ALU.add).then_inc(cV.h, 1)
        add = (cV, cV.inc())
        B.xblk_free[slot] = add
        B.lns_free[1 - lslot] = add

        dst_sem = B.dST[lslot]

        @block.sync
        def _(sy, tok0=tok0, lns=lns):
            _w(sy, add)
            sy.dma_start(out=dst.ap()[tok0:tok0 + 128, :],
                         in_=af[:, lns:lns + 1024]).then_inc(dst_sem.h, 16)
        B.lns_free[lslot] = (dst_sem, dst_sem.inc(16))

    B.xn_free = [last_h_mm, last_q_mm]
    B.wi_free = last_h_mm
    B.wo_free = last_o_mm


_PERM = None


def _perm():
    global _PERM
    if _PERM is None:
        p = np.zeros(1024, dtype=np.int64)
        for h in range(16):
            for d in range(64):
                p[h * 64 + d] = d * 16 + h
        _PERM = p
    return _PERM


_NC_CACHE = {}


def _get_program(phases):
    key = tuple(phases)
    if key not in _NC_CACHE:
        _NC_CACHE[key] = build_program(phases)
    return _NC_CACHE[key]


def kernel(**inputs):
    ins = {k: np.ascontiguousarray(np.asarray(v)) for k, v in inputs.items()}
    perm = _perm()
    Wq = ins['L0_attn_Wq'][:, perm]
    Wk = ins['L0_attn_Wk'][:, perm]
    Wv = ins['L0_attn_Wv'][:, perm]
    Wh = ins['L0_attn_Wh'][:, perm]
    Wo = ins['L0_attn_Wo'][perm, :]
    bq = ins['L0_attn_bq'][perm]
    bk = ins['L0_attn_bk'][perm]
    bv = ins['L0_attn_bv'][perm]
    bh = ins['L0_attn_bh'][perm]
    beta = ins['L0_attn_beta'][perm]
    lnw = ins['L0_attn_ln_w'][perm]
    lnb = ins['L0_attn_ln_b'][perm]

    brows = np.zeros((128, 2048), np.float32)
    brows[0, 0:1024] = bk
    brows[32, 0:1024] = bv
    brows[64, 0:1024] = ins['L0_attn_bo']
    brows[0, 1024:2048] = bh
    brows[32, 1024:2048] = beta
    lnrows = np.zeros((128, 1024), np.float32)
    lnrows[0] = lnw
    lnrows[32] = lnb
    bcols = np.zeros((128, 16), np.float32)
    bcols[:, 0:8] = bq.reshape(8, 128).T
    bcols[:, 8:10] = ins['L0_ffn_bg1'].reshape(2, 128).T
    bcols[:, 10] = 1e-5
    ident = np.eye(128, dtype=ml_dtypes.bfloat16)

    x = ins['L0_attn_x']
    nc = _get_program(('A', 'B', 'C'))
    in_maps = []
    for c in range(NCORE):
        sl = slice(c * GTOK, (c + 1) * GTOK)
        xcore = np.ascontiguousarray(
            np.concatenate([x[0, sl], x[1, sl]], axis=0))
        in_maps.append(dict(
            xcore=xcore,
            w_in=ins['L0_ffn_w_in'], w_out=ins['L0_ffn_w_out'],
            wg1=ins['L0_ffn_wg1'], wg2=ins['L0_ffn_wg2'],
            wq=np.ascontiguousarray(Wq), wk=np.ascontiguousarray(Wk),
            wv=np.ascontiguousarray(Wv), wh=np.ascontiguousarray(Wh),
            wo_a=np.ascontiguousarray(Wo),
            brows=brows, lnrows=lnrows, bcols=bcols, ident=ident))
    res = run_bass_kernel_spmd(nc, in_maps, core_ids=list(range(NCORE)))
    out = np.empty((2, 16384, 1024), np.float32)
    for c in range(NCORE):
        xo = res.results[c]['xout']
        out[0, c * GTOK:(c + 1) * GTOK] = xo[0:GTOK]
        out[1, c * GTOK:(c + 1) * GTOK] = xo[GTOK:TOK]
    return out


# version 04
# speedup vs baseline: 1.0830x; 1.0830x over previous
import numpy as np
import ml_dtypes
import concourse.bass as bass
import concourse.mybir as mybir
from concourse.bass_utils import run_bass_kernel_spmd

F32 = mybir.dt.float32
F16 = mybir.dt.float16
BF16 = mybir.dt.bfloat16
AF = mybir.ActivationFunctionType
ALU = mybir.AluOpType
AX = mybir.AxisListType

NCORE = 8
TOK = 4096
NGRP = 2
GTOK = TOK // NGRP
NBLK = GTOK // 128
NTIL = GTOK // 512
D = 1024
DC = D // 128
HID = 2048
HC = HID // 128
NE = 8

XN_O = 0
WI_O = 16384
WO_O = 32768
HT_O = 49152
XNT_O = 57344
KSB_O = 59392
K_SB_O = KSB_O + 2080
V_SB_O = KSB_O + 3104
GFM_O = KSB_O + 2080
GTM_O = KSB_O + 4160
BROW_O = 65664
KVBF_O = 67712
BCB_O = 68752
ONE_O = 70800
IDT_O = 70928
ABF_COLS = 71056

XB_O = 0
LNS_O = 3072
LNS2_O = 4096
KVS_O = 5120
LNR_O = 6160
ONF_O = 7184
MV_O = 7312
RSD_O = 7344
RSD2_O = 7360
ST6_O = 7376
GATES_O = 7392
LOGI_O = 7520
M1_O = 7648
M2_O = 7664
MB_O = 7680
EQ_O = 7808
EX_O = 7936
SS_O = 8064
RH_O = 8080
DEN_O = 8096
RDN_O = 8112
BCL_O = 8128
AF32_COLS = 8160

AH_COLS = 16384


class Ctr:
    def __init__(self, h):
        self.h = h
        self.n = 0

    def inc(self, k=1):
        self.n += k
        return self.n

    def mark(self):
        return (self, self.n)

    def reserve(self, k=1):
        self.n += k
        return (self, self.n)


def _w(eng, mark):
    if mark is None:
        return
    if isinstance(mark, list):
        for m in mark:
            _w(eng, m)
        return
    ctr, val = mark
    assert val <= ctr.n, "wait on not-yet-emitted semaphore value"
    if val > 0:
        eng.wait_ge(ctr.h, val)


class Builder:
    pass


def build_program(phases=('A', 'B', 'C')):
    nc = bass.Bass("TRN2", target_bir_lowering=False, debug=False)

    dr = {}
    dr['xin'] = nc.dram_tensor("xcore", [TOK, D], F32, kind="ExternalInput")
    dr['xout'] = nc.dram_tensor("xout", [TOK, D], F32, kind="ExternalOutput")
    dr['w_in'] = nc.dram_tensor("w_in", [NE, D, HID], F32, kind="ExternalInput")
    dr['w_out'] = nc.dram_tensor("w_out", [NE, HID, D], F32, kind="ExternalInput")
    dr['wg1'] = nc.dram_tensor("wg1", [D, 256], F32, kind="ExternalInput")
    dr['wg2'] = nc.dram_tensor("wg2", [256, NE], F32, kind="ExternalInput")
    for w in ['wq', 'wk', 'wv', 'wh', 'wo_a']:
        dr[w] = nc.dram_tensor(w, [D, D], F32, kind="ExternalInput")
    dr['brows'] = nc.dram_tensor("brows", [128, 2048], F32, kind="ExternalInput")
    dr['lnrows'] = nc.dram_tensor("lnrows", [128, 1024], F32, kind="ExternalInput")
    dr['bcols'] = nc.dram_tensor("bcols", [128, 16], F32, kind="ExternalInput")
    dr['ident'] = nc.dram_tensor("ident", [128, 128], BF16, kind="ExternalInput")
    dr['x1'] = nc.dram_tensor("x1", [TOK, D], F32)
    dr['x2'] = nc.dram_tensor("x2", [TOK, D], F32)
    dr['xnscr'] = nc.dram_tensor("xnscr", [NGRP, 128, 16384], BF16)
    dr['cc_in'] = nc.dram_tensor("cc_in", [128, 1056], F32)
    dr['cc_out'] = nc.dram_tensor("cc_out", [128, 1056], F32)

    from contextlib import ExitStack
    with ExitStack() as st:
        B = Builder()
        B.nc = nc
        B.abf = st.enter_context(nc.sbuf_tensor("abf", [128, ABF_COLS], BF16))
        B.af = st.enter_context(nc.sbuf_tensor("af32", [128, AF32_COLS], F32))
        B.ah = st.enter_context(nc.sbuf_tensor("ah", [128, AH_COLS], F16))
        B.kvs = st.enter_context(nc.sbuf_tensor("kvs", [128, 1056], F32))
        B.pb = [st.enter_context(nc.psum_tensor(f"pb{i}", [128, 512], F32))
                for i in range(7)]
        B.pbt = st.enter_context(nc.psum_tensor("pbt", [128, 1024], BF16))
        B.pbx = B.pbt[:, :].bitcast(F32)
        B.cT = Ctr(st.enter_context(nc.semaphore("sT")))
        B.cV = Ctr(st.enter_context(nc.semaphore("sV")))
        B.cS = Ctr(st.enter_context(nc.semaphore("sS")))
        B.cG = Ctr(st.enter_context(nc.semaphore("sG")))
        B.cCC = Ctr(st.enter_context(nc.semaphore("sCC")))
        B.dC = Ctr(st.enter_context(nc.semaphore("dC")))
        B.dWI = Ctr(st.enter_context(nc.semaphore("dWI")))
        B.dWO = Ctr(st.enter_context(nc.semaphore("dWO")))
        B.dXB = [Ctr(st.enter_context(nc.semaphore(f"dXB{i}"))) for i in range(3)]
        B.dST = [Ctr(st.enter_context(nc.semaphore(f"dST{i}"))) for i in range(2)]
        B.dXN = Ctr(st.enter_context(nc.semaphore("dXN")))
        B.dCC = Ctr(st.enter_context(nc.semaphore("dCC")))
        B.block = st.enter_context(nc.Block())

        B.bank_free = {i: None for i in range(8)}
        B.xblk_free = [None, None, None]
        B.xnt_free = [None, None]
        B.lns_free = [None, None]
        B.gfm_free = [None, None]
        B.gtm_free = [None, None]
        B.wi_free = None
        B.wo_free = None
        B.wi_ready = None
        B.wo_ready = None
        B.xn_free = []
        B.aggr_prev = None
        B.src_barrier = []

        _setup(B, dr)
        if 'A' in phases:
            dst = dr['x1'] if ('B' in phases or 'C' in phases) else dr['xout']
            ffn_phase(B, dr, src=dr['xin'], dst=dst)
        if 'B' in phases or 'b' in phases:
            src = dr['x1'] if 'A' in phases else dr['xin']
            dst = dr['x2'] if 'C' in phases else dr['xout']
            attn_phase(B, dr, src=src, dst=dst, b1_only=('b' in phases))
        if 'C' in phases:
            src = dr['x2'] if 'B' in phases else (dr['x1'] if 'A' in phases else dr['xin'])
            ffn_phase(B, dr, src=src, dst=dr['xout'])

        @B.block.sync
        def _(sy):
            for c in [B.dC, B.dWI, B.dWO, B.dXN, B.dCC] + B.dXB + B.dST:
                if c.n:
                    sy.wait_ge(c.h, c.n)
            if B.cCC.n:
                sy.wait_ge(B.cCC.h, B.cCC.n)
    return nc


def _setup(B, dr):
    abf, af, pb = B.abf, B.af, B.pb
    block = B.block
    cT, cV, dC = B.cT, B.cV, B.dC

    @block.gpsimd
    def _(g):
        g.dma_start(out=abf[:, BROW_O:BROW_O + 2048], in_=dr['brows'].ap()).then_inc(dC.h, 16)
        g.dma_start(out=af[:, LNR_O:LNR_O + 1024], in_=dr['lnrows'].ap()).then_inc(dC.h, 16)
        g.dma_start(out=af[:, BCL_O:BCL_O + 16], in_=dr['bcols'].ap()).then_inc(dC.h, 16)
        g.dma_start(out=abf[:, IDT_O:IDT_O + 128], in_=dr['ident'].ap()).then_inc(dC.h, 16)
        g.dma_start(out=abf[:, KSB_O:KSB_O + 2048].rearrange("p (c g) -> p c g", c=DC),
                    in_=dr['wg1'].ap().rearrange("(c p) g -> p c g", p=128)).then_inc(dC.h, 16)
        g.dma_start(out=abf[:, KSB_O + 2048:KSB_O + 2064].rearrange("p (c g) -> p c g", c=2),
                    in_=dr['wg2'].ap().rearrange("(c p) g -> p c g", p=128)).then_inc(dC.h, 16)
    dC.inc(6 * 16)
    B.consts = dC.mark()

    @block.vector
    def _(v):
        v.memset(abf[:, ONE_O:ONE_O + 128], 1.0)
        v.memset(af[:, ONF_O:ONF_O + 128], 1.0).then_inc(cV.h, 1)
    ones_done = (cV, cV.inc())

    @block.tensor
    def _(t):
        _w(t, B.consts)
        _w(t, ones_done)
        for j, pbase in enumerate([0, 32]):
            for nt in range(2):
                t.matmul(pb[j * 2 + nt][:, 0:512],
                         af[pbase:pbase + 1, ONF_O:ONF_O + 128],
                         af[pbase:pbase + 1, LNR_O + nt * 512:LNR_O + (nt + 1) * 512],
                         start=True, stop=True).then_inc(cT.h, 1)
                cT.inc()
    bc_mm = cT.mark()

    @block.vector
    def _(v):
        _w(v, bc_mm)
        for j in range(2):
            for nt in range(2):
                ins = v.tensor_copy(abf[:, BCB_O + j * 1024 + nt * 512:BCB_O + j * 1024 + (nt + 1) * 512],
                                    pb[j * 2 + nt][:, 0:512])
        ins.then_inc(cV.h, 1)
    bc_done = (cV, cV.inc())
    for i in range(4):
        B.bank_free[i] = bc_done


def ln_stats(B, dr, src, g):
    af = B.af
    block = B.block
    cV, cS = B.cV, B.cS

    for blk in range(NBLK):
        slot = blk % 3
        dxb = B.dXB[slot]
        tok0 = g * GTOK + blk * 128

        @block.sync
        def _(sy, blk=blk, slot=slot, tok0=tok0):
            if blk == 0:
                _w(sy, B.src_barrier)
            _w(sy, B.xblk_free[slot])
            sy.dma_start(out=af[:, XB_O + slot * 1024:XB_O + (slot + 1) * 1024],
                         in_=src.ap()[tok0:tok0 + 128, :]).then_inc(dxb.h, 16)
        ld = (dxb, dxb.inc(16))

        @block.vector
        def _(v, blk=blk, slot=slot):
            _w(v, ld)
            if blk == 0:
                _w(v, B.consts)
            _w(v, B.aggr_prev)
            xb = af[:, XB_O + slot * 1024:XB_O + (slot + 1) * 1024]
            v.bn_stats(af[:, ST6_O:ST6_O + 6], xb[:, 0:512]).then_inc(cV.h, 1)
            v.wait_ge(cV.h, cV.inc())
            v.bn_stats(af[:, ST6_O + 6:ST6_O + 12], xb[:, 512:1024]).then_inc(cV.h, 1)
            v.wait_ge(cV.h, cV.inc())
            v.bn_aggr(af[:, MV_O + blk * 2:MV_O + blk * 2 + 2],
                      af[:, ST6_O:ST6_O + 12]).then_inc(cV.h, 1)
        aggr = (cV, cV.inc())
        B.aggr_prev = aggr
        B.xblk_free[slot] = aggr

    @block.scalar
    def _(s):
        _w(s, B.aggr_prev)
        s.activation(af[:, RSD_O:RSD_O + 16],
                     af[:, MV_O:MV_O + 32].rearrange("p (b two) -> p b two", two=2)[:, :, 1],
                     AF.Sqrt, bias=af[:, BCL_O + 10:BCL_O + 11], scale=1.0).then_inc(cS.h, 1)
    sq = (cS, cS.inc())

    @block.vector
    def _(v):
        _w(v, sq)
        v.reciprocal(af[:, RSD2_O:RSD2_O + 16], af[:, RSD_O:RSD_O + 16]).then_inc(cV.h, 1)
        v.wait_ge(cV.h, cV.inc())
    return cV.mark()


def ln_norm(B, dr, src, g, rstd_done):
    abf, af, pbt = B.abf, B.af, B.pbt
    block = B.block
    cT, cV = B.cT, B.cV

    for blk in range(NBLK):
        slot = blk % 3
        dxb = B.dXB[slot]
        tok0 = g * GTOK + blk * 128

        @block.sync
        def _(sy, slot=slot, tok0=tok0):
            _w(sy, B.xblk_free[slot])
            sy.dma_start(out=af[:, XB_O + slot * 1024:XB_O + (slot + 1) * 1024],
                         in_=src.ap()[tok0:tok0 + 128, :]).then_inc(dxb.h, 16)
        ld = (dxb, dxb.inc(16))

        @block.vector
        def _(v, blk=blk, slot=slot):
            _w(v, ld)
            if blk == 0:
                _w(v, rstd_done)
            tslot = blk % 2
            _w(v, B.xnt_free[tslot])
            xb = af[:, XB_O + slot * 1024:XB_O + (slot + 1) * 1024]
            v.tensor_scalar(abf[:, XNT_O + tslot * 1024:XNT_O + (tslot + 1) * 1024],
                            xb, af[:, MV_O + blk * 2:MV_O + blk * 2 + 1],
                            af[:, RSD2_O + blk:RSD2_O + blk + 1],
                            op0=ALU.subtract, op1=ALU.mult).then_inc(cV.h, 1)
        norm = (cV, cV.inc())
        B.xblk_free[slot] = norm

        @block.tensor
        def _(t, blk=blk, g=g):
            tslot = blk % 2
            _w(t, norm)
            _w(t, B.bank_free[7])
            if blk == 0:
                _w(t, B.consts)
            for dc in range(DC):
                ins = t.transpose(pbt[:, dc * 128:(dc + 1) * 128],
                                  abf[:, XNT_O + tslot * 1024 + dc * 128:
                                      XNT_O + tslot * 1024 + (dc + 1) * 128],
                                  abf[:, IDT_O:IDT_O + 128])
            ins.then_inc(cT.h, 1)
        tp = (cT, cT.inc())

        @block.vector
        def _(v, blk=blk):
            _w(v, tp)
            if blk == 0:
                _w(v, B.xn_free)
            v.tensor_copy(abf[:, XN_O:XN_O + 16384]
                          .rearrange("p (c n) -> p c n", c=DC)[:, :, blk * 128:(blk + 1) * 128],
                          pbt[:, 0:1024].rearrange("p (c n) -> p c n", c=DC)).then_inc(cV.h, 1)
        ev = (cV, cV.inc())
        B.bank_free[7] = ev
        B.xnt_free[blk % 2] = ev
    B.xn_free = []
    return (cV, cV.n)


def ln_pass(B, dr, src, g):
    rstd = ln_stats(B, dr, src, g)
    return ln_norm(B, dr, src, g, rstd)


def ffn_phase(B, dr, src, dst):
    abf = B.abf
    block = B.block
    dWI, dWO = B.dWI, B.dWO
    B.src_barrier = [B.dST[0].mark(), B.dST[1].mark()]

    def dma_wi(e):
        @block.gpsimd
        def _(gp):
            _w(gp, B.wi_free)
            for half in range(2):
                gp.dma_start(
                    out=abf[:, WI_O + half * 8192:WI_O + (half + 1) * 8192]
                    .rearrange("p (c h) -> p c h", c=4),
                    in_=dr['w_in'].ap()[e].rearrange("(c p) h -> p c h", p=128)
                    [:, half * 4:(half + 1) * 4, :]).then_inc(dWI.h, 16)
        B.wi_ready = (dWI, dWI.inc(32))

    def dma_wo(e):
        @block.gpsimd
        def _(gp):
            _w(gp, B.wo_free)
            for half in range(2):
                gp.dma_start(
                    out=abf[:, WO_O + half * 8192:WO_O + (half + 1) * 8192]
                    .rearrange("p (c h) -> p c h", c=8),
                    in_=dr['w_out'].ap()[e].rearrange("(c p) h -> p c h", p=128)
                    [:, half * 8:(half + 1) * 8, :]).then_inc(dWO.h, 16)
        B.wo_ready = (dWO, dWO.inc(32))

    dma_wi(0)
    dma_wo(0)

    rstd = ln_stats(B, dr, src, 0)
    xn_ready = ln_norm(B, dr, src, 0, rstd)
    for g in range(NGRP):
        gates_done = _emit_gating(B, g, xn_ready)
        for e in range(NE):
            _emit_expert(B, g, e, gates_done, xn_ready, dma_wi, dma_wo)
            if e == NE - 2 and g + 1 < NGRP:
                rstd = ln_stats(B, dr, src, g + 1)
        if g + 1 < NGRP:
            xn_next = ln_norm(B, dr, src, g + 1, rstd)
        _emit_residual(B, src, dst, g)
        if g + 1 < NGRP:
            xn_ready = xn_next


def _emit_gating(B, g, xn_ready):
    abf, af, pb = B.abf, B.af, B.pb
    block = B.block
    cT, cV, cS = B.cT, B.cV, B.cS

    mm_marks = []
    gfm_marks = [cS.reserve() for _ in range(2)]
    for tp in range(2):
        @block.tensor
        def _(t, tp=tp):
            if tp == 0:
                _w(t, xn_ready)
                _w(t, B.consts)
            for bk in range(2, 6):
                _w(t, B.bank_free[bk])
            for dc in range(DC):
                for j in range(4):
                    mc, ti = j % 2, 2 * tp + j // 2
                    ins = t.matmul(pb[2 + j][:, 0:512],
                                   abf[:, KSB_O + dc * 256 + mc * 128:KSB_O + dc * 256 + (mc + 1) * 128],
                                   abf[:, XN_O + dc * 2048 + ti * 512:XN_O + dc * 2048 + (ti + 1) * 512],
                                   start=(dc == 0), stop=(dc == DC - 1))
            ins.then_inc(cT.h, 1)
        mm = (cT, cT.inc())
        mm_marks.append(mm)
        for bk in range(2, 6):
            B.bank_free[bk] = gfm_marks[tp]

    @block.scalar
    def _(s):
        for tp in range(2):
            _w(s, mm_marks[tp])
            for j in range(4):
                mc, ti = j % 2, 2 * tp + j // 2
                ins = s.activation(abf[:, HT_O + mc * 2048 + ti * 512:HT_O + mc * 2048 + (ti + 1) * 512],
                                   pb[2 + j][:, 0:512], AF.Gelu,
                                   bias=af[:, BCL_O + 8 + mc:BCL_O + 9 + mc], scale=1.0)
            ins.then_inc(cS.h, 1)

    @block.tensor
    def _(t):
        _w(t, gfm_marks)
        _w(t, B.bank_free[4])
        for blk in range(NBLK):
            for gc in range(2):
                ins = t.matmul(pb[4][:, blk * 8:(blk + 1) * 8],
                               abf[:, HT_O + gc * 2048 + blk * 128:HT_O + gc * 2048 + (blk + 1) * 128],
                               abf[:, KSB_O + 2048 + gc * 8:KSB_O + 2048 + (gc + 1) * 8],
                               start=(gc == 0), stop=(gc == 1))
        ins.then_inc(cT.h, 1)
    lg_mm = (cT, cT.inc())

    def vstep(v, ins):
        ins.then_inc(cV.h, 1)
        v.wait_ge(cV.h, cV.inc())

    @block.vector
    def _(v):
        _w(v, lg_mm)
        logi = af[:, LOGI_O:LOGI_O + 128]
        logi3 = logi.rearrange("p (b e) -> p b e", e=8)
        mb = af[:, MB_O:MB_O + 128]
        mb3 = mb.rearrange("p (b e) -> p b e", e=8)
        eq = af[:, EQ_O:EQ_O + 128]
        ex = af[:, EX_O:EX_O + 128]
        vstep(v, v.tensor_copy(logi, pb[4][:, 0:128]))
        vstep(v, v.tensor_reduce(af[:, M1_O:M1_O + 16], logi3, axis=AX.X, op=ALU.max))
        vstep(v, v.tensor_copy(mb3, af[:, M1_O:M1_O + 16].unsqueeze(2).broadcast_to([128, 16, 8])))
        vstep(v, v.tensor_tensor(eq, logi, mb, op=ALU.is_equal))
        vstep(v, v.scalar_tensor_tensor(ex, eq, -1e30, logi, op0=ALU.mult, op1=ALU.add))
        vstep(v, v.tensor_reduce(af[:, M2_O:M2_O + 16],
                                 ex.rearrange("p (b e) -> p b e", e=8), axis=AX.X, op=ALU.max))
        vstep(v, v.tensor_copy(mb3, af[:, M2_O:M2_O + 16].unsqueeze(2).broadcast_to([128, 16, 8])))
        vstep(v, v.tensor_tensor(eq, logi, mb, op=ALU.is_ge))
    pre_exp = cV.mark()

    @block.scalar
    def _(s):
        _w(s, pre_exp)
        s.activation(af[:, EX_O:EX_O + 128], af[:, LOGI_O:LOGI_O + 128], AF.Exp).then_inc(cS.h, 1)
    exp_done = (cS, cS.inc())

    @block.vector
    def _(v):
        _w(v, exp_done)
        ex = af[:, EX_O:EX_O + 128]
        mb = af[:, MB_O:MB_O + 128]
        vstep(v, v.tensor_reduce(af[:, SS_O:SS_O + 16],
                                 ex.rearrange("p (b e) -> p b e", e=8), axis=AX.X, op=ALU.add))
        vstep(v, v.tensor_scalar(af[:, SS_O:SS_O + 16], af[:, SS_O:SS_O + 16],
                                 2.0, None, op0=ALU.mult))
        vstep(v, v.reciprocal(af[:, RH_O:RH_O + 16], af[:, SS_O:SS_O + 16]))
        vstep(v, v.tensor_copy(mb.rearrange("p (b e) -> p b e", e=8),
                               af[:, RH_O:RH_O + 16].unsqueeze(2).broadcast_to([128, 16, 8])))
        vstep(v, v.tensor_tensor(af[:, LOGI_O:LOGI_O + 128], ex, mb, op=ALU.mult))
        v.tensor_tensor(af[:, GATES_O:GATES_O + 128], af[:, LOGI_O:LOGI_O + 128],
                        af[:, EQ_O:EQ_O + 128], op=ALU.mult).then_inc(cV.h, 1)
    gates_done = (cV, cV.inc())
    B.bank_free[4] = gates_done
    return gates_done


def _emit_expert(B, g, e, gates_done, xn_ready, dma_wi, dma_wo):
    abf, af, ah, pb = B.abf, B.af, B.ah, B.pb
    block = B.block
    cT, cV, cS = B.cT, B.cV, B.cS

    wi_ready = B.wi_ready
    wo_ready = B.wo_ready
    last_h_mm = None
    last_o_mm = None

    for ti in range(NTIL):
        hmm_marks = []
        hev_marks = [cS.reserve() for _ in range(4)]
        for q in range(4):
            base = 4 * (q % 2)

            @block.tensor
            def _(t, ti=ti, q=q, base=base):
                if ti == 0 and q == 0:
                    _w(t, wi_ready)
                    _w(t, xn_ready)
                for j in range(4):
                    _w(t, B.bank_free[base + j])
                for dc in range(DC):
                    x_sl = abf[:, XN_O + dc * 2048 + ti * 512:XN_O + dc * 2048 + (ti + 1) * 512]
                    for j in range(4):
                        hc = 4 * q + j
                        ins = t.matmul(_obank(B, base + j),
                                       abf[:, WI_O + dc * 2048 + hc * 128:WI_O + dc * 2048 + (hc + 1) * 128],
                                       x_sl, start=(dc == 0), stop=(dc == DC - 1))
                ins.then_inc(cT.h, 1)
            mm = (cT, cT.inc())
            hmm_marks.append(mm)
            for j in range(4):
                B.bank_free[base + j] = hev_marks[q]
            if ti == NTIL - 1 and q == 3:
                last_h_mm = mm

        @block.scalar
        def _(s, ti=ti):
            for q in range(4):
                base = 4 * (q % 2)
                _w(s, hmm_marks[q])
                for j in range(4):
                    hc = 4 * q + j
                    ins = s.activation(abf[:, HT_O + hc * 512:HT_O + (hc + 1) * 512],
                                       _obank(B, base + j), AF.Gelu)
                ins.then_inc(cS.h, 1)

        oev_marks = [cV.reserve() for _ in range(2)]
        omm_marks = []
        for q in range(2):
            base = 4 * (q % 2)

            @block.tensor
            def _(t, ti=ti, q=q, base=base):
                _w(t, hev_marks[2])
                if q == 0 and ti == 0:
                    _w(t, wo_ready)
                    _w(t, gates_done)
                for j in range(4):
                    _w(t, B.bank_free[base + j])
                for kc in range(HC):
                    if kc == 12:
                        _w(t, hev_marks[3])
                    for j in range(4):
                        sb = 2 * q + j // 2
                        half = j % 2
                        ins = t.matmul(_obank(B, base + j),
                                       abf[:, HT_O + kc * 512 + sb * 128:HT_O + kc * 512 + (sb + 1) * 128],
                                       abf[:, WO_O + kc * 1024 + half * 512:WO_O + kc * 1024 + (half + 1) * 512],
                                       start=(kc == 0), stop=(kc == HC - 1))
                ins.then_inc(cT.h, 1)
            omm = (cT, cT.inc())
            omm_marks.append(omm)
            for j in range(4):
                B.bank_free[base + j] = oev_marks[q]
            if ti == NTIL - 1 and q == 1:
                last_o_mm = omm

        @block.vector
        def _(v, ti=ti, e=e):
            for q in range(2):
                base = 4 * (q % 2)
                _w(v, omm_marks[q])
                for j in range(4):
                    sb = 2 * q + j // 2
                    half = j % 2
                    blkb = ti * 4 + sb
                    ya = ah[:, blkb * 1024 + half * 512:blkb * 1024 + (half + 1) * 512]
                    gate = af[:, GATES_O + blkb * 8 + e:GATES_O + blkb * 8 + e + 1]
                    if e == 0:
                        ins = v.tensor_scalar(ya, _obank(B, base + j), gate, None, op0=ALU.mult)
                    else:
                        ins = v.scalar_tensor_tensor(ya, _obank(B, base + j), gate, ya,
                                                     op0=ALU.mult, op1=ALU.add)
                ins.then_inc(cV.h, 1)

    B.wi_free = last_h_mm
    B.wo_free = last_o_mm
    if e == NE - 1:
        if g + 1 < NGRP:
            dma_wi(0)
            dma_wo(0)
    else:
        dma_wi(e + 1)
        dma_wo(e + 1)
    B.ya_done = cV.mark()
    B.xn_free = [last_h_mm]


def _obank(B, bk):
    return B.pbx if bk == 7 else B.pb[bk][:, 0:512]


def _emit_residual(B, src, dst, g):
    af, ah = B.af, B.ah
    block = B.block
    cV = B.cV
    ya_done = B.ya_done

    for blk in range(NBLK):
        slot = blk % 3
        dxb = B.dXB[slot]
        tok0 = g * GTOK + blk * 128

        @block.sync
        def _(sy, blk=blk, slot=slot, tok0=tok0):
            _w(sy, B.xblk_free[slot])
            sy.dma_start(out=af[:, XB_O + slot * 1024:XB_O + (slot + 1) * 1024],
                         in_=src.ap()[tok0:tok0 + 128, :]).then_inc(dxb.h, 16)
        ld = (dxb, dxb.inc(16))

        lslot = blk % 2
        lns = LNS_O if lslot == 0 else LNS2_O

        @block.vector
        def _(v, blk=blk, slot=slot, lns=lns, lslot=lslot):
            _w(v, ld)
            if blk == 0:
                _w(v, ya_done)
            _w(v, B.lns_free[lslot])
            v.tensor_tensor(af[:, lns:lns + 1024],
                            af[:, XB_O + slot * 1024:XB_O + (slot + 1) * 1024],
                            ah[:, blk * 1024:(blk + 1) * 1024], op=ALU.add).then_inc(cV.h, 1)
        add = (cV, cV.inc())
        B.xblk_free[slot] = add

        dst_sem = B.dST[lslot]

        @block.sync
        def _(sy, tok0=tok0, lns=lns):
            _w(sy, add)
            sy.dma_start(out=dst.ap()[tok0:tok0 + 128, :],
                         in_=af[:, lns:lns + 1024]).then_inc(dst_sem.h, 16)
        B.lns_free[lslot] = (dst_sem, dst_sem.inc(16))


def attn_phase(B, dr, src, dst, b1_only=False):
    B.src_barrier = [B.dST[0].mark(), B.dST[1].mark()]
    abf, af, pb, kvs = B.abf, B.af, B.pb, B.kvs
    block = B.block
    cT, cV, cS, cCC = B.cT, B.cV, B.cS, B.cCC
    dWI, dWO, dXN, dCC = B.dWI, B.dWO, B.dXN, B.dCC

    @block.gpsimd
    def _(gp):
        _w(gp, B.wo_free)
        for name, off in [('wk', WO_O), ('wv', WO_O + 8192)]:
            gp.dma_start(out=abf[:, off:off + 8192].rearrange("p (c h) -> p c h", c=DC),
                         in_=dr[name].ap().rearrange("(c p) h -> p c h", p=128)).then_inc(dWO.h, 16)
    wkv_ready = (dWO, dWO.inc(32))

    @block.gpsimd
    def _(gp):
        _w(gp, B.wi_free)
        for name, off in [('wq', WI_O), ('wh', WI_O + 8192)]:
            gp.dma_start(out=abf[:, off:off + 8192].rearrange("p (c h) -> p c h", c=DC),
                         in_=dr[name].ap().rearrange("(c p) h -> p c h", p=128)).then_inc(dWI.h, 16)
    wqh_ready = (dWI, dWI.inc(32))

    kv_marks = []
    kv_last_mm = None
    for g in range(NGRP):
        xn_ready = ln_pass(B, dr, src, g)

        @block.sync
        def _(sy, g=g):
            _w(sy, xn_ready)
            sy.dma_start(out=dr['xnscr'].ap()[g], in_=abf[:, XN_O:XN_O + 16384]).then_inc(dXN.h, 16)
        xn_saved = (dXN, dXN.inc(16))

        @block.vector
        def _(v, g=g):
            v.memset(kvs[:, g * 528:(g + 1) * 528], 0.0).then_inc(cV.h, 1)
        kv_acc = (cV, cV.inc())

        kvev = [cS.reserve() for _ in range(NBLK)]
        pmm_marks = []
        kv_acc_list = []
        for blk in range(NBLK):
            @block.tensor
            def _(t, blk=blk):
                if blk == 0:
                    _w(t, wkv_ready)
                    _w(t, xn_ready)
                for bk in range(4):
                    _w(t, B.bank_free[bk])
                for dc in range(DC):
                    x_sl = abf[:, XN_O + dc * 2048 + blk * 128:XN_O + dc * 2048 + (blk + 1) * 128]
                    for j in range(4):
                        woff = WO_O + (j // 2) * 8192 + dc * 1024 + (j % 2) * 512
                        t.matmul(pb[j][:, 0:512], x_sl, abf[:, woff:woff + 512],
                                 start=(dc == 0), stop=False)
                for j in range(4):
                    pbase = 0 if j < 2 else 32
                    ins = t.matmul(pb[j][:, 0:512],
                                   abf[pbase:pbase + 1, ONE_O:ONE_O + 128],
                                   abf[pbase:pbase + 1, BROW_O + (j % 2) * 512:BROW_O + (j % 2) * 512 + 512],
                                   start=False, stop=True)
                ins.then_inc(cT.h, 1)
            pmm = (cT, cT.inc())
            pmm_marks.append(pmm)
            for bk in range(4):
                B.bank_free[bk] = kvev[blk]

            kvbank = 4 + (blk % 2)

            @block.tensor
            def _(t, blk=blk, kvbank=kvbank):
                _w(t, kvev[blk])
                _w(t, B.bank_free[kvbank])
                _w(t, B.bank_free[6])
                for h in range(16):
                    hp, par = h // 2, h % 2
                    lhs = abf[:, K_SB_O + h * 64:K_SB_O + (h + 1) * 64]
                    t.matmul(pb[kvbank][par * 64:par * 64 + 64, hp * 64:(hp + 1) * 64],
                             lhs, abf[:, V_SB_O + h * 64:V_SB_O + (h + 1) * 64],
                             start=True, stop=True, tile_position=(0, 64 * par))
                    ins = t.matmul(pb[6][par * 64:par * 64 + 64, (blk % 2) * 8 + hp:(blk % 2) * 8 + hp + 1],
                                   lhs, abf[:, ONE_O:ONE_O + 1],
                                   start=True, stop=True, tile_position=(0, 64 * par))
                ins.then_inc(cT.h, 1)
            kvmm = (cT, cT.inc())
            if g == NGRP - 1 and blk == NBLK - 1:
                kv_last_mm = kvmm

            @block.vector
            def _(v, blk=blk, g=g, kvbank=kvbank):
                _w(v, kvmm)
                v.tensor_tensor(kvs[:, g * 528:g * 528 + 512],
                                kvs[:, g * 528:g * 528 + 512],
                                pb[kvbank][:, 0:512], op=ALU.add)
                ins = v.tensor_tensor(kvs[:, g * 528 + 512:g * 528 + 520],
                                      kvs[:, g * 528 + 512:g * 528 + 520],
                                      pb[6][:, (blk % 2) * 8:(blk % 2) * 8 + 8], op=ALU.add)
                ins.then_inc(cV.h, 1)
            kv_acc = (cV, cV.inc())
            B.bank_free[kvbank] = kv_acc
            B.bank_free[6] = kv_acc

        @block.scalar
        def _(s, g=g):
            for blk in range(NBLK):
                _w(s, pmm_marks[blk])
                for half in range(2):
                    s.activation(abf[:, K_SB_O + half * 512:K_SB_O + (half + 1) * 512],
                                 pb[half][:, 0:512], AF.Sigmoid)
                for half in range(2):
                    ins = s.copy(abf[:, V_SB_O + half * 512:V_SB_O + (half + 1) * 512],
                                 pb[2 + half][:, 0:512])
                ins.then_inc(cS.h, 1)

        kv_marks.append(kv_acc)
        B.xn_free = [kv_last_mm if kv_last_mm else kvmm, xn_saved]

    @block.sync
    def _(sy):
        _w(sy, kv_marks)
        sy.dma_start(out=dr['cc_in'].ap(), in_=kvs[:, 0:1056]).then_inc(dCC.h, 16)
    cc_st = (dCC, dCC.inc(16))

    @block.gpsimd
    def _(gp):
        _w(gp, cc_st)
        gp.collective_compute(
            "AllReduce", ALU.add,
            replica_groups=[list(range(NCORE))],
            ins=[dr['cc_in'].ap().opt()],
            outs=[dr['cc_out'].ap().opt()],
        ).then_inc(cCC.h, 1)
    cc_done = (cCC, cCC.inc())

    @block.sync
    def _(sy):
        _w(sy, cc_done)
        sy.dma_start(out=kvs[:, 0:1056], in_=dr['cc_out'].ap()).then_inc(dCC.h, 16)
    cc_ld = (dCC, dCC.inc(16))

    @block.vector
    def _(v):
        _w(v, cc_ld)
        for g in range(NGRP):
            v.tensor_copy(abf[:, KVBF_O + g * 520:KVBF_O + (g + 1) * 520]
                          .rearrange("p (hp c) -> p hp c", c=65)[:, :, 0:64],
                          kvs[:, g * 528:g * 528 + 512].rearrange("p (hp c) -> p hp c", c=64))
            ins = v.tensor_copy(abf[:, KVBF_O + g * 520:KVBF_O + (g + 1) * 520]
                                .rearrange("p (hp c) -> p hp c", c=65)[:, :, 64:65],
                                kvs[:, g * 528 + 512:g * 528 + 520].unsqueeze(2))
        ins.then_inc(cV.h, 1)
    kvbf = (cV, cV.inc())

    if b1_only:
        @block.sync
        def _(sy):
            _w(sy, kvbf)
            sy.dma_start(out=dst.ap()[0:128, :], in_=kvs[:, 0:1024]).then_inc(B.dST[0].h, 16)
            sy.dma_start(out=dst.ap()[128:256, 0:32], in_=kvs[:, 1024:1056]).then_inc(B.dST[0].h, 16)
        B.dST[0].inc(32)
        return

    @block.gpsimd
    def _(gp):
        _w(gp, [kv_last_mm])
        gp.dma_start(out=abf[:, WO_O:WO_O + 8192].rearrange("p (c h) -> p c h", c=DC),
                     in_=dr['wo_a'].ap().rearrange("(c p) h -> p c h", p=128)).then_inc(dWO.h, 16)
    woa_ready = (dWO, dWO.inc(16))

    for g in range(NGRP):
        _emit_b2(B, dr, src, dst, g, kvbf, wqh_ready, woa_ready)


def _emit_b2(B, dr, src, dst, g, kvbf, wqh_ready, woa_ready):
    abf, af, ah, pb, pbt = B.abf, B.af, B.ah, B.pb, B.pbt
    block = B.block
    cT, cV, cS = B.cT, B.cV, B.cS
    dXN = B.dXN

    def qfm(oc, c0, c1):
        if oc < 4:
            base = WO_O + 8192 + oc * 2048
        else:
            base = HT_O + (oc - 4) * 2048
        return abf[:, base + c0:base + c1]

    @block.sync
    def _(sy, g=g):
        _w(sy, B.xn_free)
        sy.dma_start(out=abf[:, XN_O:XN_O + 16384], in_=dr['xnscr'].ap()[g]).then_inc(dXN.h, 16)
    xn_ld = (dXN, dXN.inc(16))

    q_marks = []
    last_q_mm = None
    qsec = 0
    qev_marks = []
    qmm_marks = []
    qspec = []
    for op2 in range(DC // 2):
        for tp2 in range(NTIL // 2):
            qev_marks.append(cS.reserve())
    for op2 in range(DC // 2):
        for tp2 in range(NTIL // 2):
            base = 4 * (qsec % 2)

            @block.tensor
            def _(t, op2=op2, tp2=tp2, base=base, qsec=qsec):
                if qsec == 0:
                    _w(t, xn_ld)
                    _w(t, wqh_ready)
                for j in range(4):
                    _w(t, B.bank_free[base + j])
                for kc in range(DC):
                    for j in range(4):
                        oc = 2 * op2 + j // 2
                        ti = 2 * tp2 + j % 2
                        ins = t.matmul(_obank(B, base + j),
                                       abf[:, WI_O + kc * 1024 + oc * 128:WI_O + kc * 1024 + (oc + 1) * 128],
                                       abf[:, XN_O + kc * 2048 + ti * 512:XN_O + kc * 2048 + (ti + 1) * 512],
                                       start=(kc == 0), stop=(kc == DC - 1))
                ins.then_inc(cT.h, 1)
            mm = (cT, cT.inc())
            qmm_marks.append(mm)
            qspec.append((op2, tp2, base))
            last_q_mm = mm
            for j in range(4):
                B.bank_free[base + j] = qev_marks[qsec]
            qsec += 1
    q_marks = qev_marks

    @block.scalar
    def _(s):
        for i, (op2, tp2, base) in enumerate(qspec):
            _w(s, qmm_marks[i])
            for j in range(4):
                oc = 2 * op2 + j // 2
                ti = 2 * tp2 + j % 2
                ins = s.activation(qfm(oc, ti * 512, (ti + 1) * 512), _obank(B, base + j),
                                   AF.Sigmoid, bias=af[:, BCL_O + oc:BCL_O + oc + 1], scale=1.0)
            ins.then_inc(cS.h, 1)

    rdn_marks = []
    for blk in range(NBLK):
        @block.tensor
        def _(t, blk=blk, g=g):
            if blk == 0:
                _w(t, kvbf)
                _w(t, q_marks)
            for j in range(4):
                _w(t, B.bank_free[j])
            ti, sb = blk // 4, blk % 4
            for hp in range(8):
                for par in range(2):
                    bank = par * 2 + hp // 4
                    s = hp % 4
                    lhs = qfm(hp, ti * 512 + sb * 128, ti * 512 + (sb + 1) * 128)
                    ins = t.matmul(
                        pb[bank][:, s * 65:s * 65 + 65],
                        lhs[par * 64:par * 64 + 64, :],
                        abf[par * 64:par * 64 + 64,
                            KVBF_O + g * 520 + hp * 65:KVBF_O + g * 520 + (hp + 1) * 65],
                        start=True, stop=True, tile_position=(64 * par, 0))
            ins.then_inc(cT.h, 1)
        nmm = (cT, cT.inc())

        @block.vector
        def _(v, blk=blk):
            _w(v, nmm)
            for j in range(4):
                ins = v.tensor_copy(af[:, DEN_O + j * 4:DEN_O + (j + 1) * 4],
                                    pb[j][:, 0:260].rearrange("p (s c) -> p s c", c=65)[:, :, 64])
            ins.then_inc(cV.h, 1)
            v.wait_ge(cV.h, cV.inc())
            v.reciprocal(af[:, RDN_O:RDN_O + 16], af[:, DEN_O:DEN_O + 16]).then_inc(cV.h, 1)
            v.wait_ge(cV.h, cV.inc())
            for h in range(16):
                par, hp = h % 2, h // 2
                bank = par * 2 + hp // 4
                s = hp % 4
                ins = v.tensor_scalar(ah[:, blk * 1024 + h * 64:blk * 1024 + (h + 1) * 64],
                                      pb[bank][:, s * 65:s * 65 + 64],
                                      af[:, RDN_O + bank * 4 + s:RDN_O + bank * 4 + s + 1],
                                      None, op0=ALU.mult)
            ins.then_inc(cV.h, 1)
        rdn_done = (cV, cV.inc())
        rdn_marks.append(rdn_done)

        @block.vector
        def _(v, blk=blk, rdn_done=rdn_done):
            _w(v, rdn_done)
            _w(v, B.aggr_prev)
            ab = ah[:, blk * 1024:(blk + 1) * 1024]
            v.bn_stats(af[:, ST6_O:ST6_O + 6], ab[:, 0:512]).then_inc(cV.h, 1)
            v.wait_ge(cV.h, cV.inc())
            v.bn_stats(af[:, ST6_O + 6:ST6_O + 12], ab[:, 512:1024]).then_inc(cV.h, 1)
            v.wait_ge(cV.h, cV.inc())
            v.bn_aggr(af[:, MV_O + blk * 2:MV_O + blk * 2 + 2],
                      af[:, ST6_O:ST6_O + 12]).then_inc(cV.h, 1)
        aggr = (cV, cV.inc())
        B.aggr_prev = aggr
        for j in range(4):
            B.bank_free[j] = aggr

    @block.scalar
    def _(s):
        _w(s, B.aggr_prev)
        s.activation(af[:, RSD_O:RSD_O + 16],
                     af[:, MV_O:MV_O + 32].rearrange("p (b two) -> p b two", two=2)[:, :, 1],
                     AF.Sqrt, bias=af[:, BCL_O + 10:BCL_O + 11], scale=1.0).then_inc(cS.h, 1)
    sq = (cS, cS.inc())

    @block.vector
    def _(v):
        _w(v, sq)
        v.reciprocal(af[:, RSD2_O:RSD2_O + 16], af[:, RSD_O:RSD_O + 16]).then_inc(cV.h, 1)
        v.wait_ge(cV.h, cV.inc())
    rstd_done = cV.mark()

    last_h_mm = None
    last_o_mm = None
    for blk in range(NBLK):
        tok0 = g * GTOK + blk * 128
        slot = blk % 3
        dxb = B.dXB[slot]

        @block.sync
        def _(sy, blk=blk, slot=slot, tok0=tok0):
            if blk == 0:
                _w(sy, B.src_barrier)
            _w(sy, B.xblk_free[slot])
            sy.dma_start(out=af[:, XB_O + slot * 1024:XB_O + (slot + 1) * 1024],
                         in_=src.ap()[tok0:tok0 + 128, :]).then_inc(dxb.h, 16)
        xld = (dxb, dxb.inc(16))

        lslot = blk % 2
        lns = LNS_O if lslot == 0 else LNS2_O
        lns2 = LNS2_O if lslot == 0 else LNS_O

        @block.vector
        def _(v, blk=blk, lns=lns, lslot=lslot):
            if blk == 0:
                _w(v, rstd_done)
            _w(v, B.lns_free[lslot])
            v.tensor_scalar(af[:, lns:lns + 1024], ah[:, blk * 1024:(blk + 1) * 1024],
                            af[:, MV_O + blk * 2:MV_O + blk * 2 + 1],
                            af[:, RSD2_O + blk:RSD2_O + blk + 1],
                            op0=ALU.subtract, op1=ALU.mult).then_inc(cV.h, 1)
        t1 = (cV, cV.inc())

        cG = B.cG

        @block.gpsimd
        def _(gp, blk=blk, lns=lns, lns2=lns2, lslot=lslot):
            _w(gp, t1)
            _w(gp, B.lns_free[1 - lslot])
            gp.tensor_tensor(af[:, lns2:lns2 + 1024], af[:, lns:lns + 1024],
                             abf[:, BCB_O:BCB_O + 1024], op=ALU.mult).then_inc(cG.h, 1)
            gp.wait_ge(cG.h, cG.inc())
            gp.tensor_tensor(af[:, lns:lns + 1024], af[:, lns2:lns2 + 1024],
                             abf[:, BCB_O + 1024:BCB_O + 2048], op=ALU.add).then_inc(cG.h, 1)
        ln_done = (cG, cG.inc())

        @block.tensor
        def _(t, blk=blk):
            _w(t, B.bank_free[5])
            _w(t, B.bank_free[6])
            for half in range(2):
                t.matmul(pb[5 + half][:, 0:512],
                         abf[32:33, ONE_O:ONE_O + 128],
                         abf[32:33, BROW_O + 1024 + half * 512:BROW_O + 1024 + (half + 1) * 512],
                         start=True, stop=False)
            for dc in range(DC):
                x_sl = abf[:, XN_O + dc * 2048 + blk * 128:XN_O + dc * 2048 + (blk + 1) * 128]
                for half in range(2):
                    t.matmul(pb[5 + half][:, 0:512], x_sl,
                             abf[:, WI_O + 8192 + dc * 1024 + half * 512:
                                 WI_O + 8192 + dc * 1024 + (half + 1) * 512],
                             start=False, stop=False)
            for half in range(2):
                ins = t.matmul(pb[5 + half][:, 0:512],
                               abf[0:1, ONE_O:ONE_O + 128],
                               abf[0:1, BROW_O + 1024 + half * 512:BROW_O + 1024 + (half + 1) * 512],
                               start=False, stop=True)
            ins.then_inc(cT.h, 1)
        hmm = (cT, cT.inc())
        last_h_mm = hmm

        hev = cS.reserve()
        relu_mark = cS.reserve()
        xnt_free_prev = B.xnt_free[blk % 2]
        lns2_free = B.lns_free[1 - lslot]
        B.bank_free[5] = hev
        B.bank_free[6] = hev

        @block.vector
        def _(v, blk=blk, lns=lns):
            _w(v, ln_done)
            _w(v, hev)
            _w(v, B.gtm_free[blk % 2])
            v.tensor_tensor(abf[:, GTM_O + (blk % 2) * 1024:GTM_O + (blk % 2 + 1) * 1024],
                            af[:, lns:lns + 1024],
                            abf[:, XNT_O + (blk % 2) * 1024:XNT_O + (blk % 2 + 1) * 1024],
                            op=ALU.mult).then_inc(cV.h, 1)
        gated = (cV, cV.inc())
        B.xnt_free[blk % 2] = gated
        lns_consumed = gated

        @block.tensor
        def _(t, blk=blk):
            _w(t, gated)
            _w(t, B.bank_free[7])
            for dc in range(DC):
                ins = t.transpose(pbt[:, dc * 128:(dc + 1) * 128],
                                  abf[:, GTM_O + (blk % 2) * 1024 + dc * 128:
                                      GTM_O + (blk % 2) * 1024 + (dc + 1) * 128],
                                  abf[:, IDT_O:IDT_O + 128])
            ins.then_inc(cT.h, 1)
        tp = (cT, cT.inc())
        B.gtm_free[blk % 2] = tp

        @block.vector
        def _(v, blk=blk):
            _w(v, tp)
            _w(v, B.gfm_free[blk % 2])
            v.tensor_copy(abf[:, GFM_O + (blk % 2) * 1024:GFM_O + (blk % 2 + 1) * 1024],
                          pbt[:, 0:1024]).then_inc(cV.h, 1)
        gfm = (cV, cV.inc())
        B.bank_free[7] = gfm

        @block.tensor
        def _(t, blk=blk):
            if blk == 0:
                _w(t, woa_ready)
            _w(t, gfm)
            _w(t, B.bank_free[5])
            _w(t, B.bank_free[6])
            for dc in range(DC):
                g_sl = abf[:, GFM_O + (blk % 2) * 1024 + dc * 128:
                           GFM_O + (blk % 2) * 1024 + (dc + 1) * 128]
                for half in range(2):
                    t.matmul(pb[5 + half][:, 0:512], g_sl,
                             abf[:, WO_O + dc * 1024 + half * 512:WO_O + dc * 1024 + (half + 1) * 512],
                             start=(dc == 0), stop=False)
            for half in range(2):
                ins = t.matmul(pb[5 + half][:, 0:512],
                               abf[64:65, ONE_O:ONE_O + 128],
                               abf[64:65, BROW_O + half * 512:BROW_O + (half + 1) * 512],
                               start=False, stop=True)
            ins.then_inc(cT.h, 1)
        omm = (cT, cT.inc())
        last_o_mm = omm
        B.gfm_free[blk % 2] = omm

        @block.scalar
        def _(s, blk=blk, lns2=lns2, hmm=hmm, omm=omm,
              xnt_free_prev=xnt_free_prev, lns2_free=lns2_free):
            _w(s, hmm)
            _w(s, xnt_free_prev)
            for half in range(2):
                ins = s.copy(abf[:, XNT_O + (blk % 2) * 1024 + half * 512:
                                 XNT_O + (blk % 2) * 1024 + (half + 1) * 512],
                             pb[5 + half][:, 0:512])
            ins.then_inc(cS.h, 1)
            _w(s, omm)
            _w(s, lns2_free)
            for half in range(2):
                ins = s.activation(af[:, lns2 + half * 512:lns2 + (half + 1) * 512],
                                   pb[5 + half][:, 0:512], AF.Relu)
            ins.then_inc(cS.h, 1)
        relu = relu_mark
        B.bank_free[5] = relu
        B.bank_free[6] = relu

        @block.vector
        def _(v, blk=blk, slot=slot, lns=lns, lns2=lns2):
            _w(v, relu)
            _w(v, xld)
            _w(v, lns_consumed)
            v.tensor_tensor(af[:, lns:lns + 1024],
                            af[:, XB_O + slot * 1024:XB_O + (slot + 1) * 1024],
                            af[:, lns2:lns2 + 1024], op=ALU.add).then_inc(cV.h, 1)
        add = (cV, cV.inc())
        B.xblk_free[slot] = add
        B.lns_free[1 - lslot] = add

        dst_sem = B.dST[lslot]

        @block.sync
        def _(sy, tok0=tok0, lns=lns):
            _w(sy, add)
            sy.dma_start(out=dst.ap()[tok0:tok0 + 128, :],
                         in_=af[:, lns:lns + 1024]).then_inc(dst_sem.h, 16)
        B.lns_free[lslot] = (dst_sem, dst_sem.inc(16))

    B.xn_free = [last_h_mm, last_q_mm]
    B.wi_free = last_h_mm
    B.wo_free = last_o_mm


_PERM = None


def _perm():
    global _PERM
    if _PERM is None:
        p = np.zeros(1024, dtype=np.int64)
        for h in range(16):
            for d in range(64):
                p[h * 64 + d] = d * 16 + h
        _PERM = p
    return _PERM


_NC_CACHE = {}


def _get_program(phases):
    key = tuple(phases)
    if key not in _NC_CACHE:
        _NC_CACHE[key] = build_program(phases)
    return _NC_CACHE[key]


def kernel(**inputs):
    ins = {k: np.ascontiguousarray(np.asarray(v)) for k, v in inputs.items()}
    perm = _perm()
    Wq = ins['L0_attn_Wq'][:, perm]
    Wk = ins['L0_attn_Wk'][:, perm]
    Wv = ins['L0_attn_Wv'][:, perm]
    Wh = ins['L0_attn_Wh'][:, perm]
    Wo = ins['L0_attn_Wo'][perm, :]
    bq = ins['L0_attn_bq'][perm]
    bk = ins['L0_attn_bk'][perm]
    bv = ins['L0_attn_bv'][perm]
    bh = ins['L0_attn_bh'][perm]
    beta = ins['L0_attn_beta'][perm]
    lnw = ins['L0_attn_ln_w'][perm]
    lnb = ins['L0_attn_ln_b'][perm]

    brows = np.zeros((128, 2048), np.float32)
    brows[0, 0:1024] = bk
    brows[32, 0:1024] = bv
    brows[64, 0:1024] = ins['L0_attn_bo']
    brows[0, 1024:2048] = bh
    brows[32, 1024:2048] = beta
    lnrows = np.zeros((128, 1024), np.float32)
    lnrows[0] = lnw
    lnrows[32] = lnb
    bcols = np.zeros((128, 16), np.float32)
    bcols[:, 0:8] = bq.reshape(8, 128).T
    bcols[:, 8:10] = ins['L0_ffn_bg1'].reshape(2, 128).T
    bcols[:, 10] = 1e-5
    ident = np.eye(128, dtype=ml_dtypes.bfloat16)

    x = ins['L0_attn_x']
    nc = _get_program(('A', 'B', 'C'))
    in_maps = []
    for c in range(NCORE):
        sl = slice(c * GTOK, (c + 1) * GTOK)
        xcore = np.ascontiguousarray(
            np.concatenate([x[0, sl], x[1, sl]], axis=0))
        in_maps.append(dict(
            xcore=xcore,
            w_in=ins['L0_ffn_w_in'], w_out=ins['L0_ffn_w_out'],
            wg1=ins['L0_ffn_wg1'], wg2=ins['L0_ffn_wg2'],
            wq=np.ascontiguousarray(Wq), wk=np.ascontiguousarray(Wk),
            wv=np.ascontiguousarray(Wv), wh=np.ascontiguousarray(Wh),
            wo_a=np.ascontiguousarray(Wo),
            brows=brows, lnrows=lnrows, bcols=bcols, ident=ident))
    res = run_bass_kernel_spmd(nc, in_maps, core_ids=list(range(NCORE)))
    out = np.empty((2, 16384, 1024), np.float32)
    for c in range(NCORE):
        xo = res.results[c]['xout']
        out[0, c * GTOK:(c + 1) * GTOK] = xo[0:GTOK]
        out[1, c * GTOK:(c + 1) * GTOK] = xo[GTOK:TOK]
    return out
